# version 13
# baseline (speedup 1.0000x reference)
import contextlib

import numpy as np

import concourse.bass as bass
import concourse.mybir as mybir
from concourse.alu_op_type import AluOpType as OP
from concourse.bass_utils import run_bass_kernel_spmd

F32 = mybir.dt.float32
BF16 = mybir.dt.bfloat16
AF = mybir.ActivationFunctionType

DIM = 1024
DIN = 2048
NST = 16
NDC = DIN // 128
NKC = DIM // 128
NF = 72
SLAB = 256
EPS = 1e-5


class Ctx:

    def __init__(self, nc, es):
        self.nc = nc
        self.es = es
        self.sems = {}
        self.ledger = {}
        self.waited = {}

    def sem(self, name):
        if name not in self.sems:
            self.sems[name] = self.es.enter_context(self.nc.semaphore(name))
            self.ledger[name] = 0
        return self.sems[name]

    def inc(self, name, instr, k=1):
        instr.then_inc(self.sem(name), k)
        self.ledger[name] += k
        return instr

    def wait(self, eng, name):
        self.wait_val(eng, name, self.ledger.get(name, 0))

    def wait_val(self, eng, name, v):
        if v <= 0:
            return
        key = (id(eng), name)
        if self.waited.get(key, -1) >= v:
            return
        eng.wait_ge(self.sem(name), v)
        self.waited[key] = v

    def val(self, name):
        return self.ledger.get(name, 0)


class DmaQ:

    def __init__(self, cx, eng, name, n=8):
        self.cx = cx
        self.eng = eng
        self.names = [f"{name}{i}" for i in range(n)]
        self.n = n
        self.i = 0
        for nm in self.names:
            cx.sem(nm)

    def dma(self, out, in_, **kw):
        nm = self.names[self.i % self.n]
        self.i += 1
        self.cx.wait(self.eng, nm)
        return self.cx.inc(nm, self.eng.dma_start(out=out, in_=in_, **kw), 16)

    def wait_all(self, eng):
        for nm in self.names:
            self.cx.wait(eng, nm)


class Arena:
    def __init__(self, t, cols, name):
        self.t = t
        self.cols = cols
        self.off = 0
        self.name = name

    def take(self, n):
        assert self.off + n <= self.cols, f"{self.name} overflow {self.off}+{n}>{self.cols}"
        ap = self.t[:, self.off:self.off + n]
        self.off += n
        return ap

    def reset(self):
        self.off = 0


def build(T_OWN=4096, debug=False):
    NS = T_OWN // SLAB
    T2 = 2 * T_OWN
    nc = bass.Bass("TRN2", target_bir_lowering=False)

    def inp(name, shape):
        return nc.dram_tensor(name, shape, F32, kind="ExternalInput").ap()

    x_fm_all = inp("x_fm_all", [NKC, 128, T2 + 8])
    x_own = inp("x_own", [T_OWN, DIM])
    flag = inp("flag", [128, 1])
    win_f = inp("win_f", [128, 8 * 32 * 128])
    wout_f = inp("wout_f", [16, 128, DIM])
    wxp_f = inp("wxp_f", [16, 128, 96])
    wdt_f = inp("wdt_f", [64, 16 * 128])
    wgu_f = inp("wgu_f", [2, 128, 8 * NF * 128])
    wdown_f = inp("wdown_f", [NF, 128, DIM])
    wr_f = inp("wr_f", [8, 128, 16])
    alog_t = inp("alog_t", [128, NDC * NST])
    dpar_t = inp("dpar_t", [128, NDC])
    convw_t = inp("convw_t", [128, NDC * 4])
    convb_t = inp("convb_t", [128, NDC])
    dtb_t = inp("dtb_t", [128, NDC])
    eye_in = inp("eye_in", [128, 128])

    out = nc.dram_tensor("out", [T_OWN, DIM], F32, kind="ExternalOutput").ap()
    dbg = {}
    if debug:
        for nm, shp in [("dbg_acc", [128, 256]), ("dbg_x1", [T_OWN, DIM]),
                        ("dbg_cw", [16, T_OWN])]:
            dbg[nm] = nc.dram_tensor(nm, shp, F32, kind="ExternalOutput").ap()

    def interior(name, shape, dt):
        return nc.dram_tensor(name, shape, dt, kind="Internal").ap()

    d_winb = interior("d_winb", [128, 8 * 32 * 128], BF16)
    d_woutb = interior("d_woutb", [16, 128, DIM], BF16)
    d_wgub = interior("d_wgub", [2, 128, 8 * NF * 128], BF16)
    d_wdownb = interior("d_wdownb", [NF, 128, DIM], BF16)
    d_h = interior("d_h", [128, NDC, T2], F32)
    d_delta = interior("d_delta", [128, NDC, T2], F32)
    d_silu = interior("d_silu", [128, NDC, T_OWN], F32)
    d_B = interior("d_B", [16, T2], F32)
    d_C = interior("d_C", [16, T2], F32)
    d_y1 = interior("d_y1", [128, NDC, T_OWN], BF16)
    d_x1 = interior("d_x1", [T_OWN, DIM], F32)
    d_x1fm = interior("d_x1fm", [128, NKC, T_OWN], BF16)
    d_cw = interior("d_cw", [16, T_OWN], F32)
    d_z = interior("d_z", [128, NF, T_OWN], BF16)

    es = contextlib.ExitStack()
    A32t = es.enter_context(nc.sbuf_tensor("A32", [128, 32768], F32))
    A16t = es.enter_context(nc.sbuf_tensor("A16", [128, 22528], BF16))
    sb_wxp = es.enter_context(nc.sbuf_tensor("sb_wxp", [128, 16 * 96], F32))
    sb_wdt = es.enter_context(nc.sbuf_tensor("sb_wdt", [64, 16 * 128], F32))
    sb_wr = es.enter_context(nc.sbuf_tensor("sb_wr", [128, 8 * 16], F32))
    sb_eye = es.enter_context(nc.sbuf_tensor("sb_eye", [128, 128], F32))
    sb_negA = es.enter_context(nc.sbuf_tensor("sb_negA", [128, 256], F32))
    sb_A = es.enter_context(nc.sbuf_tensor("sb_A", [128, 256], F32))
    sb_D = es.enter_context(nc.sbuf_tensor("sb_D", [128, NDC], F32))
    sb_cw4 = es.enter_context(nc.sbuf_tensor("sb_cw4", [128, NDC * 4], F32))
    sb_cb = es.enter_context(nc.sbuf_tensor("sb_cb", [128, NDC], F32))
    sb_dtb = es.enter_context(nc.sbuf_tensor("sb_dtb", [128, NDC], F32))
    sb_ndtb = es.enter_context(nc.sbuf_tensor("sb_ndtb", [128, NDC], F32))
    sb_flag = es.enter_context(nc.sbuf_tensor("sb_flag", [128, 1], F32))
    sb_ones = es.enter_context(nc.sbuf_tensor("sb_ones", [128, 512], F32))
    sb_zero = es.enter_context(nc.sbuf_tensor("sb_zero", [128, 512], F32))
    sb_acc = es.enter_context(nc.sbuf_tensor("sb_acc", [128, 256], F32))
    sb_carry = es.enter_context(nc.sbuf_tensor("sb_carry", [128, 256], F32))
    sb_tiny = es.enter_context(nc.sbuf_tensor("sb_tiny", [128, 64], F32))

    A32 = Arena(A32t, 32768, "A32")
    A16 = Arena(A16t, 22528, "A16")

    pA = es.enter_context(nc.psum_tensor("pA", [128, 512], F32))
    pB = es.enter_context(nc.psum_tensor("pB", [128, 512], F32))
    pC = es.enter_context(nc.psum_tensor("pC", [128, 512], F32))
    pD = es.enter_context(nc.psum_tensor("pD", [128, 512], F32))
    pO = [es.enter_context(nc.psum_tensor(f"pO{i}", [128, 512], F32)) for i in range(4)]

    cx = Ctx(nc, es)
    te, sc, ve, sy, gp = nc.tensor, nc.scalar, nc.vector, nc.sync, nc.gpsimd
    QL = DmaQ(cx, sy, "qL", 8)
    QS = DmaQ(cx, gp, "qS", 8)

    with nc.Block() as blk:
        TE, SC, VE = "pTE", "pSC", "pVE"
        for dst, src in [(sb_eye[:], eye_in), (sb_negA[:], alog_t), (sb_D[:], dpar_t),
                         (sb_cw4[:], convw_t), (sb_cb[:], convb_t), (sb_dtb[:], dtb_t),
                         (sb_flag[:], flag),
                         (sb_wr[:].rearrange("p (a b) -> p a b", a=8),
                          wr_f.rearrange("a p b -> p a b")),
                         (sb_wxp[:].rearrange("p (a b) -> p a b", a=16),
                          wxp_f.rearrange("a p b -> p a b")),
                         (sb_wdt[:], wdt_f)]:
            QL.dma(out=dst, in_=src)
        QL.wait_all(sc)
        cx.wait(sc, SC)
        cx.inc(SC, sc.activation(out=sb_negA[:], in_=sb_negA[:], func=AF.Exp))
        cx.wait(ve, SC)
        cx.wait(ve, VE)
        cx.inc(VE, ve.tensor_scalar(out=sb_A[:], in0=sb_negA[:], scalar1=-1.0,
                                    scalar2=None, op0=OP.mult))
        cx.wait(ve, VE)
        cx.inc(VE, ve.memset(sb_ones[:], 1.0))
        cx.wait(ve, VE)
        cx.inc(VE, ve.memset(sb_acc[:], 0.0))
        cx.inc(VE, ve.memset(sb_zero[:], 0.0))
        cx.wait(ve, VE)
        cx.inc(VE, ve.tensor_scalar(out=sb_ndtb[:], in0=sb_dtb[:], scalar1=-1.0,
                                    scalar2=None, op0=OP.mult))

        CAST = 4096
        c32 = [A32t[:, i * CAST:(i + 1) * CAST] for i in range(4)]
        c16 = [A16t[:, i * CAST:(i + 1) * CAST] for i in range(4)]

        cast_free = [0, 0, 0, 0]

        def cast_flat(src_flat, dst_flat, total_cols, i0=0):
            n = (total_cols + CAST - 1) // CAST
            for i in range(n):
                c0 = i * CAST
                w = min(CAST, total_cols - c0)
                b = (i0 + i) % 2
                QS.wait_all(sy)
                QL.dma(out=c32[b][:, 0:w], in_=src_flat[:, c0:c0 + w])
                QL.wait_all(ve)
                cx.wait(ve, VE)
                cx.inc(VE, ve.tensor_copy(out=c16[b][:, 0:w], in_=c32[b][:, 0:w]))
                cx.wait(gp, VE)
                QS.dma(out=dst_flat[:, c0:c0 + w], in_=c16[b][:, 0:w])
            return n

        def cast_rows(src3, dst3, A, Cw):
            step = max(1, CAST // Cw)
            i0 = 0
            for a0 in range(0, A, step):
                a1 = min(a0 + step, A)
                w = (a1 - a0) * Cw
                b = i0 % 2
                i0 += 1
                QS.wait_all(sy)
                QL.dma(out=c32[b][:, 0:w].rearrange("p (a c) -> p a c", a=a1 - a0),
                    in_=src3[a0:a1].rearrange("a p c -> p a c"))
                QL.wait_all(ve)
                cx.wait(ve, VE)
                cx.inc(VE, ve.tensor_copy(out=c16[b][:, 0:w], in_=c32[b][:, 0:w]))
                cx.wait(gp, VE)
                QS.dma(out=dst3[a0:a1].rearrange("a p c -> p a c"),
                    in_=c16[b][:, 0:w].rearrange("p (a c) -> p a c", a=a1 - a0))

        cast_flat(win_f, d_winb, 8 * 32 * 128)
        cast_rows(wout_f, d_woutb, 16, DIM)
        cast_flat(wgu_f[0], d_wgub[0], 8 * NF * 128)
        cast_flat(wgu_f[1], d_wgub[1], 8 * NF * 128)
        cast_rows(wdown_f, d_wdownb, NF, DIM)

        def stage_barrier(sems):
            for e in (te, sc, ve, sy, gp):
                QL.wait_all(e)
                QS.wait_all(e)
                for s in sems:
                    cx.wait(e, s)

        stage_barrier([TE, SC, VE])

        TE, SC, VE = "aTE", "aSC", "aVE"
        A32.reset(); A16.reset()
        a_xfm = A32.take(8 * 264)
        a_h = A32.take(NDC * SLAB)
        a_xp = [A32.take(264), A32.take(264)]
        a_cv4 = [[A32.take(SLAB), A32.take(SLAB)], [A32.take(SLAB), A32.take(SLAB)]]
        a_dlt = A32.take(SLAB)
        a_bc = A32.take(SLAB)
        a_sil = A32.take(SLAB)
        a_dl = A32.take(SLAB)
        a_sig = A32.take(NDC * SLAB)
        x16 = A16.take(8 * 264)
        wtb = [A16.take(4 * 8 * 128), A16.take(4 * 8 * 128)]
        wtb_free = [0, 0]
        ps_rel_ve = [0, 0]
        ps_rel_sc = [0, 0]
        cv_rel = [0, 0]

        BTE, BSC, BVE = "bTE", "bSC", "bVE"
        SLB = 512
        NSB = T_OWN // SLB
        b_brep = A32.take(NST * SLB)
        b_dl = [A32.take(SLB), A32.take(SLB)]
        b_h = [A32.take(SLB), A32.take(SLB)]
        b_u = A32.take(SLB)
        b_S = A32.take(SLB)
        b_Dp = A32.take(SLB)
        b_E = [A32.take(SLB) for _ in range(4)]
        b_w = [A32.take(SLB) for _ in range(4)]
        b_wb = [A32.take(SLB) for _ in range(4)]
        b_junk = [A32.take(SLB), A32.take(SLB)]
        b_F = A32.take(NST)
        b_accF = A32.take(NST)
        b_ctr = A32.take(NST)
        bp_free_ve = [0, 0]
        bp_free_sc = [0, 0]
        bE_free = [0]

        def gen_bp(sl):
            QS.wait_all(sy)
            t0 = sl * SLB
            cx.wait(sy, BSC); cx.wait(sy, BVE)
            QL.dma(out=b_brep[0:1, :], in_=d_B[:, t0:t0 + SLB])
            cx.wait(te, BVE)
            for g in range(NST):
                QL.wait_all(te); cx.wait(te, BSC)
                cx.inc(BTE, te.matmul(pO[g % 2][:, 0:SLB],
                                     sb_ones[0:1, 0:128],
                                     b_brep[0:1, g * SLB:(g + 1) * SLB],
                                     start=True, stop=True))
                cx.wait(sc, BTE)
                cx.inc(BSC, sc.activation(out=b_brep[:, g * SLB:(g + 1) * SLB],
                                         in_=pO[g % 2][:, 0:SLB],
                                         func=AF.Copy))
                yield
            for dc in range(NDC):
                bb = dc % 2
                cx.wait_val(sy, BVE, bp_free_ve[bb]); cx.wait_val(sy, BSC, bp_free_sc[bb])
                QL.dma(out=b_dl[bb], in_=d_delta[:, dc, t0:t0 + SLB])
                QL.dma(out=b_h[bb], in_=d_h[:, dc, t0:t0 + SLB])
                QL.wait_all(ve)
                cx.wait(ve, BVE)
                cx.inc(BVE, ve.scalar_tensor_tensor(out=b_u[:], in0=b_dl[bb], scalar=1.0,
                                                   in1=b_h[bb], op0=OP.mult, op1=OP.mult))
                cx.inc(BVE, ve.tensor_tensor_scan(out=b_S[:], data0=sb_ones[:, 0:SLB],
                                                 data1=b_dl[bb], initial=0.0,
                                                 op0=OP.mult, op1=OP.add))
                cx.wait(ve, BVE)
                cx.inc(BVE, ve.tensor_scalar(out=b_Dp[:], in0=b_S[:],
                                            scalar1=b_S[:, SLB - 1:SLB],
                                            scalar2=None, op0=OP.subtract))
                cx.wait(sc, BVE); cx.wait(sc, BSC)
                cx.inc(BSC, sc.activation(out=b_F[:], in_=sb_negA[:, dc * NST:(dc + 1) * NST],
                                         func=AF.Exp, scale=b_S[:, SLB - 1:SLB]))
                cx.wait(ve, BSC)
                cx.inc(BVE, ve.tensor_tensor(out=b_accF[:], in0=sb_acc[:, dc * NST:(dc + 1) * NST],
                                            in1=b_F[:], op=OP.mult))
                for n0 in range(0, NST, 4):
                    cx.wait_val(sc, BVE, bE_free[0])
                    for j in range(4):
                        cx.inc(BSC, sc.activation(out=b_E[j][:], in_=b_Dp[:], func=AF.Exp,
                                                 scale=sb_A[:, dc * NST + n0 + j:dc * NST + n0 + j + 1]))
                    cx.wait(ve, BSC); cx.wait(ve, BVE)
                    for j in range(4):
                        cx.inc(BVE, ve.scalar_tensor_tensor(out=b_w[j][:], in0=b_E[j][:],
                                                           scalar=1.0, in1=b_u[:],
                                                           op0=OP.mult, op1=OP.mult))
                    bE_free[0] = cx.val(BVE)
                    cx.wait(ve, BVE)
                    for j in range(4):
                        cx.inc(BVE, ve.scalar_tensor_tensor(
                            out=b_wb[j][:], in0=b_w[j][:], scalar=1.0,
                            in1=b_brep[:, (n0 + j) * SLB:(n0 + j + 1) * SLB],
                            op0=OP.mult, op1=OP.mult,
                            accum_out=b_ctr[:, n0 + j:n0 + j + 1]))
                    yield
                cx.wait(ve, BVE)
                cx.inc(BVE, ve.tensor_tensor(out=sb_acc[:, dc * NST:(dc + 1) * NST],
                                            in0=b_accF[:], in1=b_ctr[:], op=OP.add))
                bp_free_ve[bb] = cx.val(BVE)
                bp_free_sc[bb] = cx.val(BSC)
                yield

        pend = [None]

        def pump():
            if pend[0] is not None:
                try:
                    next(pend[0])
                except StopIteration:
                    pend[0] = None
        for sl in range(2 * NS):
            t0 = sl * SLAB
            owned = sl >= NS
            cx.wait(sy, TE)
            cx.wait(sy, VE)
            QL.dma(out=a_xfm.rearrange("p (k c) -> p k c", k=8)[:, :, 0:259],
                in_=x_fm_all[:, :, 8 + t0 - 3: 8 + t0 + SLAB].rearrange("k p c -> p k c"))
            QL.wait_all(ve)
            cx.wait(ve, VE)
            cx.inc(VE, ve.tensor_copy(out=x16.rearrange("p (k c) -> p k c", k=8)[:, :, 0:259],
                                      in_=a_xfm.rearrange("p (k c) -> p k c", k=8)[:, :, 0:259]))
            x16_ready = cx.val(VE)
            m_list = list(range(16)) + (list(range(16, 32)) if owned else [])
            for mi, m in enumerate(m_list):
                pump()
                gi = (mi // 4) % 2
                if mi % 4 == 0:
                    cx.wait_val(sy, TE, wtb_free[gi])
                    QL.dma(out=wtb[gi].rearrange("p (k mc) -> p k mc", k=8),
                        in_=d_winb.rearrange("p (k mc) -> p k mc",
                                             k=8)[:, :, m * 128:(m + 4) * 128])
                wb = None
                ps = pA if (mi % 2 == 0) else pB
                QL.wait_all(te)
                cx.wait_val(te, VE, max(ps_rel_ve[mi % 2], x16_ready))
                cx.wait_val(te, SC, ps_rel_sc[mi % 2])
                for k in range(8):
                    ins = te.matmul(ps[:, 0:259],
                                    wtb[gi][:, (k * 4 + mi % 4) * 128:(k * 4 + mi % 4 + 1) * 128],
                                    x16[:, k * 264:k * 264 + 259],
                                    start=(k == 0), stop=(k == 7))
                cx.inc(TE, ins)
                if mi % 4 == 3:
                    wtb_free[gi] = cx.val(TE)
                if m < 16:
                    dc = m
                    xp = a_xp[mi % 2]
                    a_cv = a_cv4[mi % 2]
                    cx.wait(ve, TE); cx.wait(ve, VE)
                    cx.inc(VE, ve.tensor_copy(out=xp[:, 0:259], in_=ps[:, 0:259]))
                    ps_rel_ve[mi % 2] = cx.val(VE)
                    cx.wait_val(ve, SC, cv_rel[mi % 2])
                    cx.wait(ve, VE)
                    cx.inc(VE, ve.scalar_tensor_tensor(out=a_cv[0][:], in0=xp[:, 0:SLAB],
                                                       scalar=sb_cw4[:, dc * 4:dc * 4 + 1],
                                                       in1=sb_zero[:, 0:SLAB],
                                                       op0=OP.mult, op1=OP.add))
                    cx.wait(ve, VE)
                    cx.inc(VE, ve.scalar_tensor_tensor(out=a_cv[1][:], in0=xp[:, 1:SLAB + 1],
                                                       scalar=sb_cw4[:, dc * 4 + 1:dc * 4 + 2],
                                                       in1=a_cv[0][:], op0=OP.mult, op1=OP.add))
                    cx.wait(ve, VE)
                    cx.inc(VE, ve.scalar_tensor_tensor(out=a_cv[0][:], in0=xp[:, 2:SLAB + 2],
                                                       scalar=sb_cw4[:, dc * 4 + 2:dc * 4 + 3],
                                                       in1=a_cv[1][:], op0=OP.mult, op1=OP.add))
                    cx.wait(ve, VE)
                    cx.inc(VE, ve.scalar_tensor_tensor(out=a_cv[1][:], in0=xp[:, 3:SLAB + 3],
                                                       scalar=sb_cw4[:, dc * 4 + 3:dc * 4 + 4],
                                                       in1=a_cv[0][:], op0=OP.mult, op1=OP.add))
                    cx.wait(sc, VE); QS.wait_all(sc)
                    cx.wait(sc, SC)
                    cx.inc(SC, sc.activation(out=a_h[:, dc * SLAB:(dc + 1) * SLAB],
                                             in_=a_cv[1][:], func=AF.Silu,
                                             bias=sb_cb[:, dc:dc + 1]))
                    cv_rel[mi % 2] = cx.val(SC)
                else:
                    dc = m - 16
                    cx.wait(sc, TE); QS.wait_all(sc)
                    cx.wait(sc, SC)
                    cx.inc(SC, sc.activation(out=a_sil[:], in_=ps[:, 3:259], func=AF.Silu))
                    ps_rel_sc[mi % 2] = cx.val(SC)
                    cx.wait(gp, SC)
                    QS.dma(out=d_silu[:, dc, t0 - T_OWN:t0 - T_OWN + SLAB],
                                            in_=a_sil[:])
            cx.wait(gp, SC)
            QS.dma(out=d_h[:, :, t0:t0 + SLAB],
                                    in_=a_h.rearrange("p (d c) -> p d c", d=NDC))
            cx.wait(te, SC)
            for k in range(NDC):
                ins = te.matmul(pC[0:96, 0:SLAB], sb_wxp[:, k * 96:(k + 1) * 96],
                                a_h[:, k * SLAB:(k + 1) * SLAB],
                                start=(k == 0), stop=(k == NDC - 1))
            cx.inc(TE, ins)
            cx.wait(sc, TE); QS.wait_all(sc)
            cx.wait(sc, SC)
            cx.inc(SC, sc.activation(out=a_dlt[0:64, :], in_=pC[0:64, 0:SLAB], func=AF.Copy))
            dlt_ready = cx.val(SC)
            cx.wait(sc, SC)
            cx.inc(SC, sc.activation(out=a_bc[64:96, :], in_=pC[64:96, 0:SLAB], func=AF.Copy))
            dlt_ready = max(dlt_ready, cx.val(SC))
            cx.wait(gp, SC)
            QS.dma(out=d_B[:, t0:t0 + SLAB], in_=a_bc[64:80, :])
            QS.dma(out=d_C[:, t0:t0 + SLAB], in_=a_bc[80:96, :])
            sig_rel = [0, 0]
            for m in range(NDC):
                pdt = pD if m % 2 == 0 else pC
                cx.wait_val(te, SC, max(sig_rel[m % 2], dlt_ready))
                cx.inc(TE, te.matmul(pdt[:, 0:SLAB], sb_wdt[:, m * 128:(m + 1) * 128],
                                     a_dlt[0:64, :], start=True, stop=True))
                cx.wait(sc, TE)
                cx.wait(sc, SC)
                cx.inc(SC, sc.activation(out=a_sig[:, m * SLAB:(m + 1) * SLAB], in_=pdt[:, 0:SLAB],
                                         func=AF.Sigmoid, scale=-1.0, bias=sb_ndtb[:, m:m + 1]))
                sig_rel[m % 2] = cx.val(SC)
            for m in range(NDC):
                QS.wait_all(sc)
                cx.wait(sc, SC)
                cx.inc(SC, sc.activation(out=a_dl[:], in_=a_sig[:, m * SLAB:(m + 1) * SLAB],
                                         func=AF.Ln))
                cx.wait(gp, SC)
                QS.dma(out=d_delta[:, m, t0:t0 + SLAB], in_=a_dl[:])

            if sl % 2 == 1 and sl < NS:
                while pend[0] is not None:
                    pump()
                pend[0] = gen_bp(sl // 2)
        while pend[0] is not None:
            pump()
        stage_barrier([TE, SC, VE])

        cx.wait(ve, BVE)
        cx.inc(BVE, ve.tensor_scalar(out=sb_carry[:], in0=sb_acc[:],
                                    scalar1=sb_flag[:, 0:1], scalar2=None, op0=OP.mult))
        stage_barrier([BTE, BSC, BVE])
        if debug:
            QS.dma(out=dbg["dbg_acc"], in_=sb_carry[:])
            stage_barrier([])

        TE, SC, VE = "oTE", "oSC", "oVE"
        A32.reset(); A16.reset()
        o_state = A32.take(NST * SLB)
        o_brep = A32.take(NST * SLB)
        o_crep = A32.take(NST * SLB)
        o_dl = [A32.take(SLB), A32.take(SLB)]
        o_h = [A32.take(SLB), A32.take(SLB)]
        _sil = A32.take(SLB)
        o_sil = [_sil, _sil]
        o_u = A32.take(SLB)
        o_dA = [A32.take(SLB) for _ in range(4)]
        o_dBu = [A32.take(SLB) for _ in range(4)]
        o_yac = [A32.take(SLB), A32.take(SLB)]
        o_prod = o_dBu
        o_t1 = o_dA[0]
        o_y16 = A16.take(SLB)
        sil_free = [0]
        dA_free = [0]
        bo_free_ve = [0, 0]
        bo_free_sc = [0, 0]

        for sl in range(NSB):
            t0 = T_OWN + sl * SLB
            tloc = sl * SLB
            cx.wait(sy, VE)
            QL.dma(out=o_state[0:1, 0:NST * SLB], in_=d_B[:, t0:t0 + SLB])
            for g in range(NST):
                QL.wait_all(te); cx.wait(te, SC)
                cx.inc(TE, te.matmul((pA if g % 2 == 0 else pB)[:, 0:SLB],
                                     sb_ones[0:1, 0:128],
                                     o_state[0:1, g * SLB:(g + 1) * SLB],
                                     start=True, stop=True))
                cx.wait(sc, TE)
                cx.inc(SC, sc.activation(out=o_brep[:, g * SLB:(g + 1) * SLB],
                                         in_=(pA if g % 2 == 0 else pB)[:, 0:SLB],
                                         func=AF.Copy))
            cx.wait(sy, TE)
            QL.dma(out=o_state[0:1, 0:NST * SLB], in_=d_C[:, t0:t0 + SLB])
            for g in range(NST):
                QL.wait_all(te); cx.wait(te, SC)
                cx.inc(TE, te.matmul((pA if g % 2 == 0 else pB)[:, 0:SLB],
                                     sb_ones[0:1, 0:128],
                                     o_state[0:1, g * SLB:(g + 1) * SLB],
                                     start=True, stop=True))
                cx.wait(sc, TE)
                cx.inc(SC, sc.activation(out=o_crep[:, g * SLB:(g + 1) * SLB],
                                         in_=(pA if g % 2 == 0 else pB)[:, 0:SLB],
                                         func=AF.Copy))
            for dc in range(NDC):
                bb = dc % 2
                cx.wait_val(sy, VE, bo_free_ve[bb]); cx.wait_val(sy, SC, bo_free_sc[bb])
                QL.dma(out=o_dl[bb], in_=d_delta[:, dc, t0:t0 + SLB])
                QL.dma(out=o_h[bb], in_=d_h[:, dc, t0:t0 + SLB])
                cx.wait_val(sy, VE, sil_free[0])
                QL.dma(out=o_sil[bb], in_=d_silu[:, dc, tloc:tloc + SLB])
                QL.wait_all(ve)
                cx.wait(ve, VE)
                cx.inc(VE, ve.scalar_tensor_tensor(out=o_u[:], in0=o_dl[bb], scalar=1.0,
                                                   in1=o_h[bb], op0=OP.mult, op1=OP.mult))
                QL.wait_all(sc)
                for n0 in range(0, NST, 4):
                    cx.wait_val(sc, VE, dA_free[0])
                    for j in range(4):
                        cx.inc(SC, sc.activation(
                            out=o_dA[j][:], in_=o_dl[bb], func=AF.Exp,
                            scale=sb_negA[:, dc * NST + n0 + j:dc * NST + n0 + j + 1]))
                    cx.wait(ve, SC); cx.wait(ve, VE)
                    for j in range(4):
                        cx.inc(VE, ve.scalar_tensor_tensor(
                            out=o_dBu[j][:], in0=o_u[:], scalar=1.0,
                            in1=o_brep[:, (n0 + j) * SLB:(n0 + j + 1) * SLB],
                            op0=OP.mult, op1=OP.mult))
                    cx.wait(ve, VE)
                    for j in range(4):
                        n = n0 + j
                        cx.inc(VE, ve.tensor_tensor_scan(
                            out=o_state[:, n * SLB:(n + 1) * SLB], data0=o_dA[j][:],
                            data1=o_dBu[j][:],
                            initial=sb_carry[:, dc * NST + n:dc * NST + n + 1],
                            op0=OP.mult, op1=OP.add))
                    dA_free[0] = cx.val(VE)
                    cx.wait(ve, VE)
                    for j in range(4):
                        n = n0 + j
                        cx.inc(VE, ve.scalar_tensor_tensor(
                            out=o_prod[j][:], in0=o_state[:, n * SLB:(n + 1) * SLB],
                            scalar=1.0, in1=o_crep[:, n * SLB:(n + 1) * SLB],
                            op0=OP.mult, op1=OP.mult))
                    cx.wait(ve, VE)
                    cx.inc(VE, ve.tensor_tensor(out=o_prod[0][:], in0=o_prod[0][:],
                                                in1=o_prod[1][:], op=OP.add))
                    cx.inc(VE, ve.tensor_tensor(out=o_prod[2][:], in0=o_prod[2][:],
                                                in1=o_prod[3][:], op=OP.add))
                    cx.wait(ve, VE)
                    if n0 == 0:
                        cx.inc(VE, ve.tensor_tensor(out=o_yac[0][:], in0=o_prod[0][:],
                                                    in1=o_prod[2][:], op=OP.add))
                    elif n0 == 4:
                        cx.inc(VE, ve.tensor_tensor(out=o_yac[1][:], in0=o_prod[0][:],
                                                    in1=o_prod[2][:], op=OP.add))
                    else:
                        cx.inc(VE, ve.tensor_tensor(out=o_prod[0][:], in0=o_prod[0][:],
                                                    in1=o_prod[2][:], op=OP.add))
                        cx.wait(ve, VE)
                        cx.inc(VE, ve.tensor_tensor(out=o_yac[(n0 // 4) % 2][:],
                                                    in0=o_yac[(n0 // 4) % 2][:],
                                                    in1=o_prod[0][:], op=OP.add))
                cx.wait(ve, VE)
                cx.inc(VE, ve.tensor_copy(out=sb_carry[:, dc * NST:(dc + 1) * NST],
                                          in_=o_state[:, SLB - 1::SLB]))
                cx.inc(VE, ve.tensor_tensor(out=o_yac[0][:], in0=o_yac[0][:],
                                            in1=o_yac[1][:], op=OP.add))
                cx.wait(ve, VE)
                QS.wait_all(ve)
                cx.inc(VE, ve.scalar_tensor_tensor(out=o_t1[:], in0=o_h[bb],
                                                   scalar=sb_D[:, dc:dc + 1],
                                                   in1=o_yac[0][:],
                                                   op0=OP.mult, op1=OP.subtract))
                cx.wait(ve, VE)
                cx.inc(VE, ve.scalar_tensor_tensor(out=o_y16[:], in0=o_t1[:], scalar=1.0,
                                                   in1=o_sil[bb], op0=OP.mult, op1=OP.mult))
                sil_free[0] = cx.val(VE)
                cx.wait(gp, VE)
                QS.dma(out=d_y1[:, dc, tloc:tloc + SLB], in_=o_y16[:])
                bo_free_ve[bb] = cx.val(VE)
                bo_free_sc[bb] = cx.val(SC)

        stage_barrier([TE, SC, VE])


        TE, SC, VE = "cTE", "cSC", "cVE"
        A32.reset(); A16.reset()
        c_x = A32.take(2 * DIM)
        c_x1p = A32.take(2 * DIM)
        c_fm32 = A32.take(NKC * SLAB)
        c_junk = A32.take(DIM)
        c_L = A32.take(64)
        c_cwfm = A32.take(SLAB)
        c_y1 = A16.take(NDC * SLAB)
        c_wo = [A16.take(DIM), A16.take(DIM)]
        c_fm16 = A16.take(NKC * SLAB)
        tsx = sb_tiny

        for sl in range(NS):
            t0 = sl * SLAB
            cx.wait(sy, TE); cx.wait(sy, VE); cx.wait(sy, SC)
            QL.dma(out=c_y1.rearrange("p (d c) -> p d c", d=NDC),
                                    in_=d_y1[:, :, t0:t0 + SLAB])
            QL.dma(out=c_x.rearrange("p (a c) -> p a c", a=2),
                                    in_=x_own[t0:t0 + SLAB, :].rearrange("(a p) c -> p a c", p=128))
            for k in range(16):
                wb = c_wo[k % 2]
                cx.wait(sy, TE)
                QL.dma(out=wb, in_=d_woutb[k])
                QL.wait_all(te)
                if k == 0:
                    cx.wait(te, VE)
                for tc in range(2):
                    for nh in range(2):
                        ins = te.matmul(pO[tc * 2 + nh][:, 0:512],
                                        c_y1[:, k * SLAB + tc * 128:k * SLAB + tc * 128 + 128],
                                        wb[:, nh * 512:(nh + 1) * 512],
                                        start=(k == 0), stop=(k == 15),
                                        skip_group_check=True)
                        if tc == 1 and nh == 1:
                            cx.inc(TE, ins)
            for tc in range(2):
                xtc = c_x[:, tc * DIM:(tc + 1) * DIM]
                x1tc = c_x1p[:, tc * DIM:(tc + 1) * DIM]
                cx.wait(ve, TE); QL.wait_all(ve); cx.wait(ve, SC)
                for nh in range(2):
                    cx.wait(ve, VE)
                    cx.inc(VE, ve.tensor_tensor(out=x1tc[:, nh * 512:(nh + 1) * 512],
                                                in0=pO[tc * 2 + nh][:, 0:512],
                                                in1=xtc[:, nh * 512:(nh + 1) * 512], op=OP.add))
                cx.wait(sc, VE); QS.wait_all(sc)
                cx.wait(sc, SC)
                cx.inc(SC, sc.activation(out=c_junk[:], in_=x1tc, func=AF.Square,
                                         accum_out=tsx[:, 0:1]))
                cx.wait(sc, SC)
                cx.inc(SC, sc.activation(out=c_junk[:], in_=x1tc, func=AF.Copy,
                                         accum_out=tsx[:, 1:2]))
                cx.wait(ve, SC)
                cx.wait(ve, VE)
                cx.inc(VE, ve.tensor_scalar(out=tsx[:, 2:3], in0=tsx[:, 1:2],
                                            scalar1=1.0 / DIM, scalar2=None, op0=OP.mult))
                cx.wait(ve, VE)
                cx.inc(VE, ve.tensor_tensor(out=tsx[:, 3:4], in0=tsx[:, 2:3],
                                            in1=tsx[:, 2:3], op=OP.mult))
                cx.wait(ve, VE)
                cx.inc(VE, ve.tensor_scalar(out=tsx[:, 4:5], in0=tsx[:, 3:4], scalar1=-1.0,
                                            scalar2=EPS, op0=OP.mult, op1=OP.add))
                cx.wait(sc, VE)
                cx.wait(sc, SC)
                cx.inc(SC, sc.activation(out=tsx[:, 14:15], in_=tsx[:, 0:1], func=AF.Sqrt,
                                         scale=1.0 / DIM, bias=tsx[:, 4:5]))
                cx.wait(ve, SC)
                cx.wait(ve, VE)
                cx.inc(VE, ve.reciprocal(out=tsx[:, 5:6], in_=tsx[:, 14:15]))
                cx.wait(ve, VE)
                cx.inc(VE, ve.tensor_tensor(out=tsx[:, 6:7], in0=tsx[:, 2:3],
                                            in1=tsx[:, 5:6], op=OP.mult))
                cx.wait(ve, VE)
                cx.inc(VE, ve.tensor_scalar(out=tsx[:, 7:8], in0=tsx[:, 6:7], scalar1=-1.0,
                                            scalar2=None, op0=OP.mult))
                cx.wait(sc, VE)
                cx.wait(sc, SC)
                cx.inc(SC, sc.activation(out=x1tc, in_=x1tc, func=AF.Identity,
                                         scale=tsx[:, 5:6], bias=tsx[:, 7:8]))
                cx.wait(gp, SC)
                QS.dma(out=d_x1[t0 + tc * 128: t0 + (tc + 1) * 128, :],
                                        in_=x1tc)
                for kc in range(NKC):
                    cx.wait(te, SC)
                    cx.inc(TE, te.transpose(pC[:, 0:128], x1tc[:, kc * 128:(kc + 1) * 128],
                                            sb_eye[:]))
                    cx.wait(sc, TE)
                    cx.wait(sc, SC)
                    cx.inc(SC, sc.activation(
                        out=c_fm32[:, kc * SLAB + tc * 128:kc * SLAB + tc * 128 + 128],
                        in_=pC[:, 0:128], func=AF.Copy))
            cx.wait(ve, SC); QS.wait_all(ve)
            cx.wait(ve, VE)
            cx.inc(VE, ve.tensor_copy(out=c_fm16[:], in_=c_fm32[:]))
            cx.wait(gp, VE)
            QS.dma(out=d_x1fm[:, :, t0:t0 + SLAB],
                                    in_=c_fm16.rearrange("p (k c) -> p k c", k=NKC))
            for tc in range(2):
                cx.wait(te, SC)
                for k in range(NKC):
                    ins = te.matmul(pD[:, 0:16],
                                    c_fm32[:, k * SLAB + tc * 128:k * SLAB + tc * 128 + 128],
                                    sb_wr[:, k * 16:(k + 1) * 16],
                                    start=(k == 0), stop=(k == NKC - 1))
                cx.inc(TE, ins)
                L = c_L[:, 0:16]
                cx.wait(sc, TE)
                cx.wait(sc, SC)
                cx.inc(SC, sc.activation(out=L, in_=pD[:, 0:16], func=AF.Copy))
                cx.wait(ve, SC)
                cx.wait(ve, VE)
                cx.inc(VE, ve.tensor_reduce(out=tsx[:, 8:9], in_=L, axis=mybir.AxisListType.X,
                                            op=OP.max))
                cx.wait(ve, VE)
                cx.inc(VE, ve.tensor_scalar(out=tsx[:, 9:10], in0=tsx[:, 8:9], scalar1=-1.0,
                                            scalar2=None, op0=OP.mult))
                cx.wait(sc, VE)
                cx.wait(sc, SC)
                cx.inc(SC, sc.activation(out=c_L[:, 16:32], in_=L, func=AF.Exp,
                                         bias=tsx[:, 9:10], accum_out=tsx[:, 10:11]))
                cx.wait(ve, SC)
                cx.wait(ve, VE)
                cx.inc(VE, ve.reciprocal(out=tsx[:, 11:12], in_=tsx[:, 10:11]))
                P = c_L[:, 32:48]
                cx.wait(ve, VE)
                cx.inc(VE, ve.tensor_scalar(out=P, in0=c_L[:, 16:32], scalar1=tsx[:, 11:12],
                                            scalar2=None, op0=OP.mult))
                cx.wait(ve, VE)
                cx.inc(VE, ve.tensor_reduce(out=tsx[:, 12:13], in_=P,
                                            axis=mybir.AxisListType.X, op=OP.max))
                EQ = c_L[:, 48:64]
                cx.wait(ve, VE)
                cx.inc(VE, ve.tensor_scalar(out=EQ, in0=P, scalar1=tsx[:, 12:13],
                                            scalar2=None, op0=OP.is_ge))
                cx.wait(ve, VE)
                cx.inc(VE, ve.tensor_scalar(out=EQ, in0=EQ, scalar1=-1.0, scalar2=1.0,
                                            op0=OP.mult, op1=OP.add))
                cx.wait(ve, VE)
                cx.inc(VE, ve.tensor_tensor(out=c_L[:, 16:32], in0=P, in1=EQ, op=OP.mult))
                cx.wait(ve, VE)
                cx.inc(VE, ve.tensor_reduce(out=tsx[:, 13:14], in_=c_L[:, 16:32],
                                            axis=mybir.AxisListType.X, op=OP.max))
                cx.wait(ve, VE)
                cx.inc(VE, ve.tensor_scalar(out=EQ, in0=P, scalar1=tsx[:, 13:14],
                                            scalar2=None, op0=OP.is_ge))
                cx.wait(ve, VE)
                cx.inc(VE, ve.tensor_tensor(out=P, in0=P, in1=EQ, op=OP.mult))
                cx.wait(te, VE)
                cx.inc(TE, te.transpose(pC[0:16, 0:128], P, sb_eye[:]))
                cx.wait(sc, TE)
                cx.wait(sc, SC)
                cx.inc(SC, sc.activation(out=c_cwfm[0:16, tc * 128:(tc + 1) * 128],
                                         in_=pC[0:16, 0:128], func=AF.Copy))
            cx.wait(gp, SC)
            QS.dma(out=d_cw[:, t0:t0 + SLAB], in_=c_cwfm[0:16, 0:SLAB])

        stage_barrier([TE, SC, VE])
        if debug:
            tmp = A32t[:, 0:DIM]
            for sl in range(NS):
                t0 = sl * SLAB
                for tc in range(2):
                    QS.wait_all(sy)
                    QL.dma(out=tmp, in_=d_x1[t0 + tc * 128:t0 + (tc + 1) * 128, :])
                    QL.wait_all(gp)
                    QS.dma(out=dbg["dbg_x1"][t0 + tc * 128:t0 + (tc + 1) * 128, :],
                                            in_=tmp)
                QS.wait_all(sy)
                QL.dma(out=tmp[0:16, 0:SLAB], in_=d_cw[:, t0:t0 + SLAB])
                QL.wait_all(gp)
                QS.dma(out=dbg["dbg_cw"][:, t0:t0 + SLAB],
                                        in_=tmp[0:16, 0:SLAB])
            stage_barrier([])

        TE, SC, VE = "dTE", "dSC", "dVE"
        SLD = 512
        NSD = T_OWN // SLD
        A32.reset(); A16.reset()
        g_cwf = A32.take(NST * SLD)
        g_cwr = [A32.take(SLD), A32.take(SLD)]
        g_sg2 = [A32.take(SLD), A32.take(SLD)]
        g_t1 = A32.take(SLD)
        gu_rel_sc = [0, 0]
        gu_rel_ve = [0, 0]
        g_x1 = A16.take(NKC * SLD)
        g_wt = [[A16.take(8 * 4 * 128), A16.take(8 * 4 * 128)],
                [A16.take(8 * 4 * 128), A16.take(8 * 4 * 128)]]
        g_z = A16.take(SLD)
        gwt_free = [[0, 0], [0, 0]]
        cwr_free = [0, 0]

        for sl in range(NSD):
            t0 = sl * SLD
            cx.wait(sy, TE); cx.wait(sy, VE)
            QL.dma(out=g_x1.rearrange("p (k c) -> p k c", k=NKC),
                   in_=d_x1fm[:, :, t0:t0 + SLD])
            QL.dma(out=g_cwf[0:1, :], in_=d_cw[:, t0:t0 + SLD])
            for pe in range(18):
                pp = pe % 2
                f0 = pe * 4
                for g in range(2):
                    cx.wait_val(sy, TE, gwt_free[g][pp])
                    QL.dma(out=g_wt[g][pp].rearrange("p (k f c) -> p k f c", k=8, f=4),
                           in_=d_wgub.rearrange("g p (k ff c) -> g p k ff c",
                                                k=8, ff=NF)[g, :, :, f0:f0 + 4, :])
                if pe < 16:
                    QL.wait_all(te); cx.wait_val(te, VE, cwr_free[pp])
                    cx.inc(TE, te.matmul(pC[:, 0:SLD], sb_ones[0:1, 0:128],
                                         g_cwf[0:1, pe * SLD:(pe + 1) * SLD],
                                         start=True, stop=True))
                    cx.wait(sc, TE); cx.wait(sc, SC)
                    cx.inc(SC, sc.activation(out=g_cwr[pp][:], in_=pC[:, 0:SLD], func=AF.Copy))
                for f in range(4):
                    fp = f % 2
                    pg = pO[fp * 2]
                    pu = pO[fp * 2 + 1]
                    QL.wait_all(te)
                    cx.wait_val(te, SC, gu_rel_sc[fp])
                    cx.wait_val(te, VE, gu_rel_ve[fp])
                    for k in range(NKC):
                        ins = te.matmul(pg[:, 0:SLD],
                                        g_wt[0][pp][:, (k * 4 + f) * 128:(k * 4 + f + 1) * 128],
                                        g_x1[:, k * SLD:(k + 1) * SLD],
                                        start=(k == 0), stop=(k == NKC - 1))
                    cx.inc(TE, ins)
                    cx.wait(sc, TE); cx.wait(sc, SC)
                    cx.inc(SC, sc.activation(out=g_sg2[fp][:], in_=pg[:, 0:SLD], func=AF.Silu))
                    gu_rel_sc[fp] = cx.val(SC)
                    for k in range(NKC):
                        ins = te.matmul(pu[:, 0:SLD],
                                        g_wt[1][pp][:, (k * 4 + f) * 128:(k * 4 + f + 1) * 128],
                                        g_x1[:, k * SLD:(k + 1) * SLD],
                                        start=(k == 0), stop=(k == NKC - 1))
                    cx.inc(TE, ins)
                    cx.wait(ve, TE); cx.wait(ve, SC); QS.wait_all(ve); cx.wait(ve, VE)
                    if pe < 16:
                        cx.inc(VE, ve.scalar_tensor_tensor(out=g_t1[:], in0=pu[:, 0:SLD],
                                                           scalar=1.0, in1=g_sg2[fp][:],
                                                           op0=OP.mult, op1=OP.mult))
                        cx.wait(ve, VE)
                        cx.inc(VE, ve.scalar_tensor_tensor(out=g_z[:], in0=g_t1[:], scalar=1.0,
                                                           in1=g_cwr[pp][:],
                                                           op0=OP.mult, op1=OP.mult))
                    else:
                        cx.inc(VE, ve.scalar_tensor_tensor(out=g_z[:], in0=pu[:, 0:SLD],
                                                           scalar=1.0, in1=g_sg2[fp][:],
                                                           op0=OP.mult, op1=OP.mult))
                    gu_rel_ve[fp] = cx.val(VE)
                    cx.wait(gp, VE)
                    QS.dma(out=d_z[:, f0 + f, t0:t0 + SLD], in_=g_z[:])
                gwt_free[0][pp] = cx.val(TE)
                gwt_free[1][pp] = cx.val(TE)
                cwr_free[pp] = cx.val(VE)

        stage_barrier([TE, SC, VE])


        TE, SC, VE = "eTE", "eSC", "eVE"
        A32.reset(); A16.reset()
        e_x1 = A32.take(2 * DIM)
        e_o = A32.take(2 * DIM)
        e_junk = A32.take(DIM)
        e_z = [A16.take(36 * SLAB), A16.take(36 * SLAB)]
        e_wd = [A16.take(2 * DIM), A16.take(2 * DIM)]
        ewd_free = [0, 0]
        ez_free = [0, 0]

        for sl in range(NS):
            t0 = sl * SLAB
            cx.wait(sy, VE); cx.wait(sy, SC)
            QL.dma(out=e_x1.rearrange("p (a c) -> p a c", a=2),
                                    in_=d_x1[t0:t0 + SLAB, :].rearrange("(a p) c -> p a c", p=128))
            for half in range(2):
                hb = (2 * sl + half) % 2
                cx.wait_val(sy, TE, ez_free[hb])
                QL.dma(out=e_z[hb].rearrange("p (k c) -> p k c", k=36),
                                        in_=d_z[:, half * 36:(half + 1) * 36, t0:t0 + SLAB])
                for kk in range(36):
                    k = half * 36 + kk
                    gi2 = (kk // 2) % 2
                    if kk % 2 == 0:
                        cx.wait_val(sy, TE, ewd_free[gi2])
                        QL.dma(out=e_wd[gi2].rearrange("p (a c) -> p a c", a=2),
                               in_=d_wdownb[k:k + 2].rearrange("a p c -> p a c"))
                    wb = e_wd[gi2][:, (kk % 2) * DIM:(kk % 2 + 1) * DIM]
                    QL.wait_all(te)
                    if half == 0 and kk == 0:
                        cx.wait(te, VE)
                    for tc in range(2):
                        for nh in range(2):
                            ins = te.matmul(pO[tc * 2 + nh][:, 0:512],
                                            e_z[hb][:, kk * SLAB + tc * 128:kk * SLAB + tc * 128 + 128],
                                            wb[:, nh * 512:(nh + 1) * 512],
                                            start=(half == 0 and kk == 0),
                                            stop=(half == 1 and kk == 35),
                                            skip_group_check=True)
                            if tc == 1 and nh == 1:
                                cx.inc(TE, ins)
                    if kk % 2 == 1:
                        ewd_free[gi2] = cx.val(TE)
                ez_free[hb] = cx.val(TE)
            for tc in range(2):
                x1tc = e_x1[:, tc * DIM:(tc + 1) * DIM]
                otc = e_o[:, tc * DIM:(tc + 1) * DIM]
                cx.wait(ve, TE); QL.wait_all(ve); cx.wait(ve, SC)
                for nh in range(2):
                    cx.wait(ve, VE)
                    cx.inc(VE, ve.tensor_tensor(out=otc[:, nh * 512:(nh + 1) * 512],
                                                in0=pO[tc * 2 + nh][:, 0:512],
                                                in1=x1tc[:, nh * 512:(nh + 1) * 512], op=OP.add))
                cx.wait(sc, VE); QS.wait_all(sc)
                cx.wait(sc, SC)
                cx.inc(SC, sc.activation(out=e_junk[:], in_=otc, func=AF.Square,
                                         accum_out=sb_tiny[:, 0:1]))
                cx.wait(sc, SC)
                cx.inc(SC, sc.activation(out=e_junk[:], in_=otc, func=AF.Copy,
                                         accum_out=sb_tiny[:, 1:2]))
                cx.wait(ve, SC)
                cx.wait(ve, VE)
                cx.inc(VE, ve.tensor_scalar(out=sb_tiny[:, 2:3], in0=sb_tiny[:, 1:2],
                                            scalar1=1.0 / DIM, scalar2=None, op0=OP.mult))
                cx.wait(ve, VE)
                cx.inc(VE, ve.tensor_tensor(out=sb_tiny[:, 3:4], in0=sb_tiny[:, 2:3],
                                            in1=sb_tiny[:, 2:3], op=OP.mult))
                cx.wait(ve, VE)
                cx.inc(VE, ve.tensor_scalar(out=sb_tiny[:, 4:5], in0=sb_tiny[:, 3:4],
                                            scalar1=-1.0, scalar2=EPS, op0=OP.mult, op1=OP.add))
                cx.wait(sc, VE)
                cx.wait(sc, SC)
                cx.inc(SC, sc.activation(out=sb_tiny[:, 14:15], in_=sb_tiny[:, 0:1],
                                         func=AF.Sqrt, scale=1.0 / DIM, bias=sb_tiny[:, 4:5]))
                cx.wait(ve, SC)
                cx.wait(ve, VE)
                cx.inc(VE, ve.reciprocal(out=sb_tiny[:, 5:6], in_=sb_tiny[:, 14:15]))
                cx.wait(ve, VE)
                cx.inc(VE, ve.tensor_tensor(out=sb_tiny[:, 6:7], in0=sb_tiny[:, 2:3],
                                            in1=sb_tiny[:, 5:6], op=OP.mult))
                cx.wait(ve, VE)
                cx.inc(VE, ve.tensor_scalar(out=sb_tiny[:, 7:8], in0=sb_tiny[:, 6:7],
                                            scalar1=-1.0, scalar2=None, op0=OP.mult))
                cx.wait(sc, VE)
                cx.wait(sc, SC)
                cx.inc(SC, sc.activation(out=otc, in_=otc, func=AF.Identity,
                                         scale=sb_tiny[:, 5:6], bias=sb_tiny[:, 7:8]))
                cx.wait(gp, SC)
                QS.dma(out=out[t0 + tc * 128:t0 + (tc + 1) * 128, :],
                                        in_=otc)

        QS.wait_all(gp)
        QS.wait_all(sy)

    es.close()
    return nc


def stage_inputs(inputs, T_OWN=4096, x_override=None):
    x = np.asarray(x_override if x_override is not None else inputs["L0_attn_x"])
    nseq = x.shape[0]
    win = np.asarray(inputs["L0_attn_in_proj_w"])
    conv_w = np.asarray(inputs["L0_attn_conv_w"]).reshape(DIN, 4)
    conv_b = np.asarray(inputs["L0_attn_conv_b"])
    wxp = np.asarray(inputs["L0_attn_x_proj_w"])
    wdt = np.asarray(inputs["L0_attn_dt_proj_w"])
    dtb = np.asarray(inputs["L0_attn_dt_proj_b"])
    alog = np.asarray(inputs["L0_attn_A_log"])
    dpar = np.asarray(inputs["L0_attn_D_param"])
    wout = np.asarray(inputs["L0_attn_out_proj_w"])
    wr = np.asarray(inputs["L0_ffn_Wr"])
    wg = np.asarray(inputs["L0_ffn_Wg"])
    wu = np.asarray(inputs["L0_ffn_Wu"])
    wd = np.asarray(inputs["L0_ffn_Wd"])
    sg = np.asarray(inputs["L0_ffn_Sg"])
    su = np.asarray(inputs["L0_ffn_Su"])
    sd = np.asarray(inputs["L0_ffn_Sd"])
    C = np.ascontiguousarray

    win_f = C(win.reshape(8, 128, 32, 128).transpose(1, 0, 2, 3)).reshape(128, 8 * 32 * 128)
    wout_f = C(wout.reshape(16, 128, DIM))
    wxp_f = C(wxp.reshape(16, 128, 96))
    wdt_f = C(wdt.reshape(64, 16 * 128))

    def gu_tiles(we, ws):
        e_t = we.reshape(16, 8, 128, 4, 128).transpose(2, 1, 0, 3, 4).reshape(128, 8, 64, 128)
        s_t = ws.reshape(8, 128, 8, 128).transpose(1, 0, 2, 3)
        return np.concatenate([e_t, s_t], axis=2)

    wgu_f = C(np.stack([gu_tiles(wg, sg), gu_tiles(wu, su)],
                       axis=0)).reshape(2, 128, 8 * NF * 128)
    wdown_f = C(np.concatenate([wd.reshape(64, 128, DIM), sd.reshape(8, 128, DIM)], axis=0))
    wr_f = C(wr.reshape(8, 128, 16))
    alog_t = C(alog.reshape(16, 128, 16).transpose(1, 0, 2)).reshape(128, 256)
    dpar_t = C(dpar.reshape(16, 128).transpose(1, 0))
    convw_t = C(conv_w.reshape(16, 128, 4).transpose(1, 0, 2)).reshape(128, 64)
    convb_t = C(conv_b.reshape(16, 128).transpose(1, 0))
    dtb_t = C(dtb.reshape(16, 128).transpose(1, 0))
    eye = np.eye(128, dtype=np.float32)

    shared = dict(win_f=win_f, wout_f=wout_f, wxp_f=wxp_f, wdt_f=wdt_f,
                  wgu_f=wgu_f, wdown_f=wdown_f, wr_f=wr_f, alog_t=alog_t,
                  dpar_t=dpar_t, convw_t=convw_t, convb_t=convb_t, dtb_t=dtb_t,
                  eye_in=eye)

    in_maps = []
    n_cores = 2 * nseq
    for c in range(n_cores):
        s = c // 2
        role_b = c % 2
        own = x[s, role_b * T_OWN:(role_b + 1) * T_OWN, :]
        xfm = np.zeros((NKC, 128, 2 * T_OWN + 8), dtype=np.float32)
        if role_b:
            xfm[:, :, 8:8 + T_OWN] = C(x[s, 0:T_OWN, :].T).reshape(NKC, 128, T_OWN)
        xfm[:, :, 8 + T_OWN:] = C(own.T).reshape(NKC, 128, T_OWN)
        m = dict(shared)
        m["x_fm_all"] = xfm
        m["x_own"] = C(own)
        m["flag"] = np.full((128, 1), float(role_b), dtype=np.float32)
        in_maps.append(m)
    return in_maps


def kernel(**inputs):
    T_OWN = 4096
    in_maps = stage_inputs(inputs, T_OWN)
    nc = build(T_OWN)
    res = run_bass_kernel_spmd(nc, in_maps, core_ids=list(range(8)))
    outp = np.empty((4, 8192, DIM), dtype=np.float32)
    for c in range(8):
        s = c // 2
        role_b = c % 2
        outp[s, role_b * T_OWN:(role_b + 1) * T_OWN, :] = res.results[c]["out"]
    return outp


# version 14 (final)
# speedup vs baseline: 1.0535x; 1.0535x over previous
import contextlib

import numpy as np

import concourse.bass as bass
import concourse.mybir as mybir
from concourse.alu_op_type import AluOpType as OP
from concourse.bass_utils import run_bass_kernel_spmd

F32 = mybir.dt.float32
BF16 = mybir.dt.bfloat16
AF = mybir.ActivationFunctionType

DIM = 1024
DIN = 2048
NST = 16
NDC = DIN // 128
NKC = DIM // 128
NF = 72
SLAB = 256
EPS = 1e-5


class Ctx:

    def __init__(self, nc, es):
        self.nc = nc
        self.es = es
        self.sems = {}
        self.ledger = {}
        self.waited = {}

    def sem(self, name):
        if name not in self.sems:
            self.sems[name] = self.es.enter_context(self.nc.semaphore(name))
            self.ledger[name] = 0
        return self.sems[name]

    def inc(self, name, instr, k=1):
        instr.then_inc(self.sem(name), k)
        self.ledger[name] += k
        return instr

    def wait(self, eng, name):
        self.wait_val(eng, name, self.ledger.get(name, 0))

    def wait_val(self, eng, name, v):
        if v <= 0:
            return
        key = (id(eng), name)
        if self.waited.get(key, -1) >= v:
            return
        eng.wait_ge(self.sem(name), v)
        self.waited[key] = v

    def val(self, name):
        return self.ledger.get(name, 0)


class DmaQ:

    def __init__(self, cx, eng, name, n=8):
        self.cx = cx
        self.eng = eng
        self.names = [f"{name}{i}" for i in range(n)]
        self.n = n
        self.i = 0
        for nm in self.names:
            cx.sem(nm)

    def dma(self, out, in_, **kw):
        nm = self.names[self.i % self.n]
        self.i += 1
        self.cx.wait(self.eng, nm)
        return self.cx.inc(nm, self.eng.dma_start(out=out, in_=in_, **kw), 16)

    def wait_all(self, eng):
        for nm in self.names:
            self.cx.wait(eng, nm)


class Arena:
    def __init__(self, t, cols, name):
        self.t = t
        self.cols = cols
        self.off = 0
        self.name = name

    def take(self, n):
        assert self.off + n <= self.cols, f"{self.name} overflow {self.off}+{n}>{self.cols}"
        ap = self.t[:, self.off:self.off + n]
        self.off += n
        return ap

    def reset(self):
        self.off = 0


def build(T_OWN=4096, debug=False):
    NS = T_OWN // SLAB
    T2 = 2 * T_OWN
    nc = bass.Bass("TRN2", target_bir_lowering=False)

    def inp(name, shape):
        return nc.dram_tensor(name, shape, F32, kind="ExternalInput").ap()

    x_fm_all = inp("x_fm_all", [NKC, 128, T2 + 8])
    x_own = inp("x_own", [T_OWN, DIM])
    flag = inp("flag", [128, 1])
    win_f = inp("win_f", [128, 8 * 32 * 128])
    wout_f = inp("wout_f", [16, 128, DIM])
    wxp_f = inp("wxp_f", [16, 128, 96])
    wdt_f = inp("wdt_f", [64, 16 * 128])
    wgu_f = inp("wgu_f", [2, 128, 8 * NF * 128])
    wdown_f = inp("wdown_f", [NF, 128, DIM])
    wr_f = inp("wr_f", [8, 128, 16])
    alog_t = inp("alog_t", [128, NDC * NST])
    dpar_t = inp("dpar_t", [128, NDC])
    convw_t = inp("convw_t", [128, NDC * 4])
    convb_t = inp("convb_t", [128, NDC])
    dtb_t = inp("dtb_t", [128, NDC])
    eye_in = inp("eye_in", [128, 128])

    out = nc.dram_tensor("out", [T_OWN, DIM], F32, kind="ExternalOutput").ap()
    dbg = {}
    if debug:
        for nm, shp in [("dbg_acc", [128, 256]), ("dbg_x1", [T_OWN, DIM]),
                        ("dbg_cw", [16, T_OWN])]:
            dbg[nm] = nc.dram_tensor(nm, shp, F32, kind="ExternalOutput").ap()

    def interior(name, shape, dt):
        return nc.dram_tensor(name, shape, dt, kind="Internal").ap()

    d_winb = interior("d_winb", [128, 8 * 32 * 128], BF16)
    d_woutb = interior("d_woutb", [16, 128, DIM], BF16)
    d_wgub = interior("d_wgub", [2, 128, 8 * NF * 128], BF16)
    d_wdownb = interior("d_wdownb", [NF, 128, DIM], BF16)
    d_h = interior("d_h", [128, NDC, T2], F32)
    d_delta = interior("d_delta", [128, NDC, T2], F32)
    d_silu = interior("d_silu", [128, NDC, T_OWN], F32)
    d_B = interior("d_B", [16, T2], F32)
    d_C = interior("d_C", [16, T2], F32)
    d_y1 = interior("d_y1", [128, NDC, T_OWN], BF16)
    d_x1 = interior("d_x1", [T_OWN, DIM], F32)
    d_x1fm = interior("d_x1fm", [128, NKC, T_OWN], BF16)
    d_cw = interior("d_cw", [16, T_OWN], F32)
    d_z = interior("d_z", [128, NF, T_OWN], BF16)

    es = contextlib.ExitStack()
    A32t = es.enter_context(nc.sbuf_tensor("A32", [128, 32768], F32))
    A16t = es.enter_context(nc.sbuf_tensor("A16", [128, 22528], BF16))
    sb_wxp = es.enter_context(nc.sbuf_tensor("sb_wxp", [128, 16 * 96], F32))
    sb_wdt = es.enter_context(nc.sbuf_tensor("sb_wdt", [64, 16 * 128], F32))
    sb_wr = es.enter_context(nc.sbuf_tensor("sb_wr", [128, 8 * 16], F32))
    sb_eye = es.enter_context(nc.sbuf_tensor("sb_eye", [128, 128], F32))
    sb_negA = es.enter_context(nc.sbuf_tensor("sb_negA", [128, 256], F32))
    sb_A = es.enter_context(nc.sbuf_tensor("sb_A", [128, 256], F32))
    sb_D = es.enter_context(nc.sbuf_tensor("sb_D", [128, NDC], F32))
    sb_cw4 = es.enter_context(nc.sbuf_tensor("sb_cw4", [128, NDC * 4], F32))
    sb_cb = es.enter_context(nc.sbuf_tensor("sb_cb", [128, NDC], F32))
    sb_dtb = es.enter_context(nc.sbuf_tensor("sb_dtb", [128, NDC], F32))
    sb_ndtb = es.enter_context(nc.sbuf_tensor("sb_ndtb", [128, NDC], F32))
    sb_flag = es.enter_context(nc.sbuf_tensor("sb_flag", [128, 1], F32))
    sb_ones = es.enter_context(nc.sbuf_tensor("sb_ones", [128, 512], F32))
    sb_zero = es.enter_context(nc.sbuf_tensor("sb_zero", [128, 512], F32))
    sb_acc = es.enter_context(nc.sbuf_tensor("sb_acc", [128, 256], F32))
    sb_carry = es.enter_context(nc.sbuf_tensor("sb_carry", [128, 256], F32))
    sb_tiny = es.enter_context(nc.sbuf_tensor("sb_tiny", [128, 64], F32))

    A32 = Arena(A32t, 32768, "A32")
    A16 = Arena(A16t, 22528, "A16")

    pA = es.enter_context(nc.psum_tensor("pA", [128, 512], F32))
    pB = es.enter_context(nc.psum_tensor("pB", [128, 512], F32))
    pC = es.enter_context(nc.psum_tensor("pC", [128, 512], F32))
    pD = es.enter_context(nc.psum_tensor("pD", [128, 512], F32))
    pO = [es.enter_context(nc.psum_tensor(f"pO{i}", [128, 512], F32)) for i in range(4)]

    cx = Ctx(nc, es)
    te, sc, ve, sy, gp = nc.tensor, nc.scalar, nc.vector, nc.sync, nc.gpsimd
    QL = DmaQ(cx, sy, "qL", 8)
    QS = DmaQ(cx, gp, "qS", 8)

    with nc.Block() as blk:
        TE, SC, VE = "pTE", "pSC", "pVE"
        for dst, src in [(sb_eye[:], eye_in), (sb_negA[:], alog_t), (sb_D[:], dpar_t),
                         (sb_cw4[:], convw_t), (sb_cb[:], convb_t), (sb_dtb[:], dtb_t),
                         (sb_flag[:], flag),
                         (sb_wr[:].rearrange("p (a b) -> p a b", a=8),
                          wr_f.rearrange("a p b -> p a b")),
                         (sb_wxp[:].rearrange("p (a b) -> p a b", a=16),
                          wxp_f.rearrange("a p b -> p a b")),
                         (sb_wdt[:], wdt_f)]:
            QL.dma(out=dst, in_=src)
        QL.wait_all(sc)
        cx.wait(sc, SC)
        cx.inc(SC, sc.activation(out=sb_negA[:], in_=sb_negA[:], func=AF.Exp))
        cx.wait(ve, SC)
        cx.wait(ve, VE)
        cx.inc(VE, ve.tensor_scalar(out=sb_A[:], in0=sb_negA[:], scalar1=-1.0,
                                    scalar2=None, op0=OP.mult))
        cx.wait(ve, VE)
        cx.inc(VE, ve.memset(sb_ones[:], 1.0))
        cx.wait(ve, VE)
        cx.inc(VE, ve.memset(sb_acc[:], 0.0))
        cx.inc(VE, ve.memset(sb_zero[:], 0.0))
        cx.wait(ve, VE)
        cx.inc(VE, ve.tensor_scalar(out=sb_ndtb[:], in0=sb_dtb[:], scalar1=-1.0,
                                    scalar2=None, op0=OP.mult))

        CAST = 4096
        c32 = [A32t[:, i * CAST:(i + 1) * CAST] for i in range(4)]
        c16 = [A16t[:, i * CAST:(i + 1) * CAST] for i in range(4)]

        cast_free = [0, 0, 0, 0]

        def cast_flat(src_flat, dst_flat, total_cols, i0=0):
            n = (total_cols + CAST - 1) // CAST
            for i in range(n):
                c0 = i * CAST
                w = min(CAST, total_cols - c0)
                b = (i0 + i) % 2
                QS.wait_all(sy)
                QL.dma(out=c32[b][:, 0:w], in_=src_flat[:, c0:c0 + w])
                QL.wait_all(ve)
                cx.wait(ve, VE)
                cx.inc(VE, ve.tensor_copy(out=c16[b][:, 0:w], in_=c32[b][:, 0:w]))
                cx.wait(gp, VE)
                QS.dma(out=dst_flat[:, c0:c0 + w], in_=c16[b][:, 0:w])
            return n

        def cast_rows(src3, dst3, A, Cw):
            step = max(1, CAST // Cw)
            i0 = 0
            for a0 in range(0, A, step):
                a1 = min(a0 + step, A)
                w = (a1 - a0) * Cw
                b = i0 % 2
                i0 += 1
                QS.wait_all(sy)
                QL.dma(out=c32[b][:, 0:w].rearrange("p (a c) -> p a c", a=a1 - a0),
                    in_=src3[a0:a1].rearrange("a p c -> p a c"))
                QL.wait_all(ve)
                cx.wait(ve, VE)
                cx.inc(VE, ve.tensor_copy(out=c16[b][:, 0:w], in_=c32[b][:, 0:w]))
                cx.wait(gp, VE)
                QS.dma(out=dst3[a0:a1].rearrange("a p c -> p a c"),
                    in_=c16[b][:, 0:w].rearrange("p (a c) -> p a c", a=a1 - a0))

        cast_flat(win_f, d_winb, 8 * 32 * 128)
        cast_rows(wout_f, d_woutb, 16, DIM)
        cast_flat(wgu_f[0], d_wgub[0], 8 * NF * 128)
        cast_flat(wgu_f[1], d_wgub[1], 8 * NF * 128)
        cast_rows(wdown_f, d_wdownb, NF, DIM)

        def stage_barrier(sems):
            for e in (te, sc, ve, sy, gp):
                QL.wait_all(e)
                QS.wait_all(e)
                for s in sems:
                    cx.wait(e, s)

        stage_barrier([TE, SC, VE])

        TE, SC, VE = "aTE", "aSC", "aVE"
        A32.reset(); A16.reset()
        a_xfm = A32.take(8 * 264)
        a_h = A32.take(NDC * SLAB)
        a_xp = [A32.take(264), A32.take(264)]
        a_cv4 = [[A32.take(SLAB), A32.take(SLAB)], [A32.take(SLAB), A32.take(SLAB)]]
        a_dlt = A32.take(SLAB)
        a_bc = A32.take(SLAB)
        a_sil = A32.take(SLAB)
        a_dl = A32.take(SLAB)
        a_sig = A32.take(NDC * SLAB)
        x16 = A16.take(8 * 264)
        wtb = [A16.take(4 * 8 * 128), A16.take(4 * 8 * 128)]
        wtb_free = [0, 0]
        ps_rel_ve = [0, 0]
        ps_rel_sc = [0, 0]
        cv_rel = [0, 0]

        BTE, BSC, BVE = "bTE", "bSC", "bVE"
        SLB = 512
        NSB = T_OWN // SLB
        b_brep = A32.take(NST * SLB)
        b_dl = [A32.take(SLB), A32.take(SLB)]
        b_h = [A32.take(SLB), A32.take(SLB)]
        b_u = A32.take(SLB)
        b_S = A32.take(SLB)
        b_Dp = A32.take(SLB)
        b_E = [A32.take(SLB) for _ in range(4)]
        b_w = [A32.take(SLB) for _ in range(4)]
        b_wb = [A32.take(SLB) for _ in range(4)]
        b_junk = [A32.take(SLB), A32.take(SLB)]
        b_F = A32.take(NST)
        b_accF = A32.take(NST)
        b_ctr = A32.take(NST)
        bp_free_ve = [0, 0]
        bp_free_sc = [0, 0]
        bE_free = [0]

        def gen_bp(sl):
            QS.wait_all(sy)
            t0 = sl * SLB
            cx.wait(sy, BSC); cx.wait(sy, BVE)
            QL.dma(out=b_brep[0:1, :], in_=d_B[:, t0:t0 + SLB])
            cx.wait(te, BVE)
            for g in range(NST):
                QL.wait_all(te); cx.wait(te, BSC)
                cx.inc(BTE, te.matmul(pO[g % 2][:, 0:SLB],
                                     sb_ones[0:1, 0:128],
                                     b_brep[0:1, g * SLB:(g + 1) * SLB],
                                     start=True, stop=True))
                cx.wait(sc, BTE)
                cx.inc(BSC, sc.activation(out=b_brep[:, g * SLB:(g + 1) * SLB],
                                         in_=pO[g % 2][:, 0:SLB],
                                         func=AF.Copy))
                yield
            for dc in range(NDC):
                bb = dc % 2
                cx.wait_val(sy, BVE, bp_free_ve[bb]); cx.wait_val(sy, BSC, bp_free_sc[bb])
                QL.dma(out=b_dl[bb], in_=d_delta[:, dc, t0:t0 + SLB])
                QL.dma(out=b_h[bb], in_=d_h[:, dc, t0:t0 + SLB])
                QL.wait_all(ve)
                cx.wait(ve, BVE)
                cx.inc(BVE, ve.scalar_tensor_tensor(out=b_u[:], in0=b_dl[bb], scalar=1.0,
                                                   in1=b_h[bb], op0=OP.mult, op1=OP.mult))
                cx.inc(BVE, ve.tensor_tensor_scan(out=b_S[:], data0=sb_ones[:, 0:SLB],
                                                 data1=b_dl[bb], initial=0.0,
                                                 op0=OP.mult, op1=OP.add))
                cx.wait(ve, BVE)
                cx.inc(BVE, ve.tensor_scalar(out=b_Dp[:], in0=b_S[:],
                                            scalar1=b_S[:, SLB - 1:SLB],
                                            scalar2=None, op0=OP.subtract))
                cx.wait(sc, BVE); cx.wait(sc, BSC)
                cx.inc(BSC, sc.activation(out=b_F[:], in_=sb_negA[:, dc * NST:(dc + 1) * NST],
                                         func=AF.Exp, scale=b_S[:, SLB - 1:SLB]))
                cx.wait(ve, BSC)
                cx.inc(BVE, ve.tensor_tensor(out=b_accF[:], in0=sb_acc[:, dc * NST:(dc + 1) * NST],
                                            in1=b_F[:], op=OP.mult))
                for n0 in range(0, NST, 4):
                    cx.wait_val(sc, BVE, bE_free[0])
                    for j in range(4):
                        cx.inc(BSC, sc.activation(out=b_E[j][:], in_=b_Dp[:], func=AF.Exp,
                                                 scale=sb_A[:, dc * NST + n0 + j:dc * NST + n0 + j + 1]))
                    cx.wait(ve, BSC); cx.wait(ve, BVE)
                    for j in range(4):
                        cx.inc(BVE, ve.scalar_tensor_tensor(out=b_w[j][:], in0=b_E[j][:],
                                                           scalar=1.0, in1=b_u[:],
                                                           op0=OP.mult, op1=OP.mult))
                    bE_free[0] = cx.val(BVE)
                    cx.wait(ve, BVE)
                    for j in range(4):
                        cx.inc(BVE, ve.scalar_tensor_tensor(
                            out=b_wb[j][:], in0=b_w[j][:], scalar=1.0,
                            in1=b_brep[:, (n0 + j) * SLB:(n0 + j + 1) * SLB],
                            op0=OP.mult, op1=OP.mult,
                            accum_out=b_ctr[:, n0 + j:n0 + j + 1]))
                    yield
                cx.wait(ve, BVE)
                cx.inc(BVE, ve.tensor_tensor(out=sb_acc[:, dc * NST:(dc + 1) * NST],
                                            in0=b_accF[:], in1=b_ctr[:], op=OP.add))
                bp_free_ve[bb] = cx.val(BVE)
                bp_free_sc[bb] = cx.val(BSC)
                yield

        pend = [None]

        def pump():
            if pend[0] is not None:
                try:
                    next(pend[0])
                except StopIteration:
                    pend[0] = None
        for sl in range(2 * NS):
            t0 = sl * SLAB
            owned = sl >= NS
            cx.wait(sy, TE)
            cx.wait(sy, VE)
            QL.dma(out=a_xfm.rearrange("p (k c) -> p k c", k=8)[:, :, 0:259],
                in_=x_fm_all[:, :, 8 + t0 - 3: 8 + t0 + SLAB].rearrange("k p c -> p k c"))
            QL.wait_all(ve)
            cx.wait(ve, VE)
            cx.inc(VE, ve.tensor_copy(out=x16.rearrange("p (k c) -> p k c", k=8)[:, :, 0:259],
                                      in_=a_xfm.rearrange("p (k c) -> p k c", k=8)[:, :, 0:259]))
            x16_ready = cx.val(VE)
            m_list = list(range(16)) + (list(range(16, 32)) if owned else [])
            for mi, m in enumerate(m_list):
                pump()
                gi = (mi // 4) % 2
                if mi % 4 == 0:
                    cx.wait_val(sy, TE, wtb_free[gi])
                    QL.dma(out=wtb[gi].rearrange("p (k mc) -> p k mc", k=8),
                        in_=d_winb.rearrange("p (k mc) -> p k mc",
                                             k=8)[:, :, m * 128:(m + 4) * 128])
                wb = None
                ps = pA if (mi % 2 == 0) else pB
                QL.wait_all(te)
                cx.wait_val(te, VE, max(ps_rel_ve[mi % 2], x16_ready))
                cx.wait_val(te, SC, ps_rel_sc[mi % 2])
                for k in range(8):
                    ins = te.matmul(ps[:, 0:259],
                                    wtb[gi][:, (k * 4 + mi % 4) * 128:(k * 4 + mi % 4 + 1) * 128],
                                    x16[:, k * 264:k * 264 + 259],
                                    start=(k == 0), stop=(k == 7))
                cx.inc(TE, ins)
                if mi % 4 == 3:
                    wtb_free[gi] = cx.val(TE)
                if m < 16:
                    dc = m
                    xp = a_xp[mi % 2]
                    a_cv = a_cv4[mi % 2]
                    cx.wait(ve, TE); cx.wait(ve, VE)
                    cx.inc(VE, ve.tensor_copy(out=xp[:, 0:259], in_=ps[:, 0:259]))
                    ps_rel_ve[mi % 2] = cx.val(VE)
                    cx.wait_val(ve, SC, cv_rel[mi % 2])
                    cx.wait(ve, VE)
                    cx.inc(VE, ve.scalar_tensor_tensor(out=a_cv[0][:], in0=xp[:, 0:SLAB],
                                                       scalar=sb_cw4[:, dc * 4:dc * 4 + 1],
                                                       in1=sb_zero[:, 0:SLAB],
                                                       op0=OP.mult, op1=OP.add))
                    cx.wait(ve, VE)
                    cx.inc(VE, ve.scalar_tensor_tensor(out=a_cv[1][:], in0=xp[:, 1:SLAB + 1],
                                                       scalar=sb_cw4[:, dc * 4 + 1:dc * 4 + 2],
                                                       in1=a_cv[0][:], op0=OP.mult, op1=OP.add))
                    cx.wait(ve, VE)
                    cx.inc(VE, ve.scalar_tensor_tensor(out=a_cv[0][:], in0=xp[:, 2:SLAB + 2],
                                                       scalar=sb_cw4[:, dc * 4 + 2:dc * 4 + 3],
                                                       in1=a_cv[1][:], op0=OP.mult, op1=OP.add))
                    cx.wait(ve, VE)
                    cx.inc(VE, ve.scalar_tensor_tensor(out=a_cv[1][:], in0=xp[:, 3:SLAB + 3],
                                                       scalar=sb_cw4[:, dc * 4 + 3:dc * 4 + 4],
                                                       in1=a_cv[0][:], op0=OP.mult, op1=OP.add))
                    cx.wait(sc, VE); QS.wait_all(sc)
                    cx.wait(sc, SC)
                    cx.inc(SC, sc.activation(out=a_h[:, dc * SLAB:(dc + 1) * SLAB],
                                             in_=a_cv[1][:], func=AF.Silu,
                                             bias=sb_cb[:, dc:dc + 1]))
                    cv_rel[mi % 2] = cx.val(SC)
                else:
                    dc = m - 16
                    cx.wait(sc, TE); QS.wait_all(sc)
                    cx.wait(sc, SC)
                    cx.inc(SC, sc.activation(out=a_sil[:], in_=ps[:, 3:259], func=AF.Silu))
                    ps_rel_sc[mi % 2] = cx.val(SC)
                    cx.wait(gp, SC)
                    QS.dma(out=d_silu[:, dc, t0 - T_OWN:t0 - T_OWN + SLAB],
                                            in_=a_sil[:])
            cx.wait(gp, SC)
            QS.dma(out=d_h[:, :, t0:t0 + SLAB],
                                    in_=a_h.rearrange("p (d c) -> p d c", d=NDC))
            cx.wait(te, SC)
            for k in range(NDC):
                ins = te.matmul(pC[0:96, 0:SLAB], sb_wxp[:, k * 96:(k + 1) * 96],
                                a_h[:, k * SLAB:(k + 1) * SLAB],
                                start=(k == 0), stop=(k == NDC - 1))
            cx.inc(TE, ins)
            cx.wait(sc, TE); QS.wait_all(sc)
            cx.wait(sc, SC)
            cx.inc(SC, sc.activation(out=a_dlt[0:64, :], in_=pC[0:64, 0:SLAB], func=AF.Copy))
            dlt_ready = cx.val(SC)
            cx.wait(sc, SC)
            cx.inc(SC, sc.activation(out=a_bc[64:96, :], in_=pC[64:96, 0:SLAB], func=AF.Copy))
            dlt_ready = max(dlt_ready, cx.val(SC))
            cx.wait(gp, SC)
            QS.dma(out=d_B[:, t0:t0 + SLAB], in_=a_bc[64:80, :])
            QS.dma(out=d_C[:, t0:t0 + SLAB], in_=a_bc[80:96, :])
            sig_rel = [0, 0]
            QS.wait_all(sc)
            for m in range(NDC):
                pump()
                pdt = pD if m % 2 == 0 else pC
                cx.wait_val(te, SC, max(sig_rel[m % 2], dlt_ready))
                cx.inc(TE, te.matmul(pdt[:, 0:SLAB], sb_wdt[:, m * 128:(m + 1) * 128],
                                     a_dlt[0:64, :], start=True, stop=True))
                cx.wait(sc, TE)
                cx.wait(sc, SC)
                cx.inc(SC, sc.activation(out=a_sig[:, m * SLAB:(m + 1) * SLAB], in_=pdt[:, 0:SLAB],
                                         func=AF.Sigmoid, scale=-1.0, bias=sb_ndtb[:, m:m + 1]))
                sig_rel[m % 2] = cx.val(SC)
            pump()
            cx.wait(sc, SC)
            cx.inc(SC, sc.activation(out=a_sig[:, :], in_=a_sig[:, :], func=AF.Ln))
            cx.wait(gp, SC)
            QS.dma(out=d_delta[:, :, t0:t0 + SLAB],
                   in_=a_sig.rearrange("p (d c) -> p d c", d=NDC))
            pump(); pump(); pump()

            if sl % 2 == 1 and sl < NS:
                while pend[0] is not None:
                    pump()
                pend[0] = gen_bp(sl // 2)
        while pend[0] is not None:
            pump()
        stage_barrier([TE, SC, VE])

        cx.wait(ve, BVE)
        cx.inc(BVE, ve.tensor_scalar(out=sb_carry[:], in0=sb_acc[:],
                                    scalar1=sb_flag[:, 0:1], scalar2=None, op0=OP.mult))
        stage_barrier([BTE, BSC, BVE])
        if debug:
            QS.dma(out=dbg["dbg_acc"], in_=sb_carry[:])
            stage_barrier([])

        TE, SC, VE = "oTE", "oSC", "oVE"
        A32.reset(); A16.reset()
        o_state = A32.take(NST * SLB)
        o_brep = A32.take(NST * SLB)
        o_crep = A32.take(NST * SLB)
        o_dl = [A32.take(SLB), A32.take(SLB)]
        o_h = [A32.take(SLB), A32.take(SLB)]
        _sil = A32.take(SLB)
        o_sil = [_sil, _sil]
        o_u = A32.take(SLB)
        o_dA = [A32.take(SLB) for _ in range(4)]
        o_dBu = [A32.take(SLB) for _ in range(4)]
        o_yac = [A32.take(SLB), A32.take(SLB)]
        o_prod = o_dBu
        o_t1 = o_dA[0]
        o_y16 = A16.take(SLB)
        sil_free = [0]
        dA_free = [0]
        bo_free_ve = [0, 0]
        bo_free_sc = [0, 0]

        for sl in range(NSB):
            t0 = T_OWN + sl * SLB
            tloc = sl * SLB
            cx.wait(sy, VE)
            QL.dma(out=o_state[0:1, 0:NST * SLB], in_=d_B[:, t0:t0 + SLB])
            for g in range(NST):
                QL.wait_all(te); cx.wait(te, SC)
                cx.inc(TE, te.matmul((pA if g % 2 == 0 else pB)[:, 0:SLB],
                                     sb_ones[0:1, 0:128],
                                     o_state[0:1, g * SLB:(g + 1) * SLB],
                                     start=True, stop=True))
                cx.wait(sc, TE)
                cx.inc(SC, sc.activation(out=o_brep[:, g * SLB:(g + 1) * SLB],
                                         in_=(pA if g % 2 == 0 else pB)[:, 0:SLB],
                                         func=AF.Copy))
            cx.wait(sy, TE)
            QL.dma(out=o_state[0:1, 0:NST * SLB], in_=d_C[:, t0:t0 + SLB])
            for g in range(NST):
                QL.wait_all(te); cx.wait(te, SC)
                cx.inc(TE, te.matmul((pA if g % 2 == 0 else pB)[:, 0:SLB],
                                     sb_ones[0:1, 0:128],
                                     o_state[0:1, g * SLB:(g + 1) * SLB],
                                     start=True, stop=True))
                cx.wait(sc, TE)
                cx.inc(SC, sc.activation(out=o_crep[:, g * SLB:(g + 1) * SLB],
                                         in_=(pA if g % 2 == 0 else pB)[:, 0:SLB],
                                         func=AF.Copy))
            for dc in range(NDC):
                bb = dc % 2
                cx.wait_val(sy, VE, bo_free_ve[bb]); cx.wait_val(sy, SC, bo_free_sc[bb])
                QL.dma(out=o_dl[bb], in_=d_delta[:, dc, t0:t0 + SLB])
                QL.dma(out=o_h[bb], in_=d_h[:, dc, t0:t0 + SLB])
                cx.wait_val(sy, VE, sil_free[0])
                QL.dma(out=o_sil[bb], in_=d_silu[:, dc, tloc:tloc + SLB])
                QL.wait_all(ve)
                cx.wait(ve, VE)
                cx.inc(VE, ve.scalar_tensor_tensor(out=o_u[:], in0=o_dl[bb], scalar=1.0,
                                                   in1=o_h[bb], op0=OP.mult, op1=OP.mult))
                QL.wait_all(sc)
                for n0 in range(0, NST, 4):
                    cx.wait_val(sc, VE, dA_free[0])
                    for j in range(4):
                        cx.inc(SC, sc.activation(
                            out=o_dA[j][:], in_=o_dl[bb], func=AF.Exp,
                            scale=sb_negA[:, dc * NST + n0 + j:dc * NST + n0 + j + 1]))
                    cx.wait(ve, SC); cx.wait(ve, VE)
                    for j in range(4):
                        cx.inc(VE, ve.scalar_tensor_tensor(
                            out=o_dBu[j][:], in0=o_u[:], scalar=1.0,
                            in1=o_brep[:, (n0 + j) * SLB:(n0 + j + 1) * SLB],
                            op0=OP.mult, op1=OP.mult))
                    cx.wait(ve, VE)
                    for j in range(4):
                        n = n0 + j
                        cx.inc(VE, ve.tensor_tensor_scan(
                            out=o_state[:, n * SLB:(n + 1) * SLB], data0=o_dA[j][:],
                            data1=o_dBu[j][:],
                            initial=sb_carry[:, dc * NST + n:dc * NST + n + 1],
                            op0=OP.mult, op1=OP.add))
                    dA_free[0] = cx.val(VE)
                    cx.wait(ve, VE)
                    for j in range(4):
                        n = n0 + j
                        cx.inc(VE, ve.scalar_tensor_tensor(
                            out=o_prod[j][:], in0=o_state[:, n * SLB:(n + 1) * SLB],
                            scalar=1.0, in1=o_crep[:, n * SLB:(n + 1) * SLB],
                            op0=OP.mult, op1=OP.mult))
                    cx.wait(ve, VE)
                    cx.inc(VE, ve.tensor_tensor(out=o_prod[0][:], in0=o_prod[0][:],
                                                in1=o_prod[1][:], op=OP.add))
                    cx.inc(VE, ve.tensor_tensor(out=o_prod[2][:], in0=o_prod[2][:],
                                                in1=o_prod[3][:], op=OP.add))
                    cx.wait(ve, VE)
                    if n0 == 0:
                        cx.inc(VE, ve.tensor_tensor(out=o_yac[0][:], in0=o_prod[0][:],
                                                    in1=o_prod[2][:], op=OP.add))
                    elif n0 == 4:
                        cx.inc(VE, ve.tensor_tensor(out=o_yac[1][:], in0=o_prod[0][:],
                                                    in1=o_prod[2][:], op=OP.add))
                    else:
                        cx.inc(VE, ve.tensor_tensor(out=o_prod[0][:], in0=o_prod[0][:],
                                                    in1=o_prod[2][:], op=OP.add))
                        cx.wait(ve, VE)
                        cx.inc(VE, ve.tensor_tensor(out=o_yac[(n0 // 4) % 2][:],
                                                    in0=o_yac[(n0 // 4) % 2][:],
                                                    in1=o_prod[0][:], op=OP.add))
                cx.wait(ve, VE)
                cx.inc(VE, ve.tensor_copy(out=sb_carry[:, dc * NST:(dc + 1) * NST],
                                          in_=o_state[:, SLB - 1::SLB]))
                cx.inc(VE, ve.tensor_tensor(out=o_yac[0][:], in0=o_yac[0][:],
                                            in1=o_yac[1][:], op=OP.add))
                cx.wait(ve, VE)
                QS.wait_all(ve)
                cx.inc(VE, ve.scalar_tensor_tensor(out=o_t1[:], in0=o_h[bb],
                                                   scalar=sb_D[:, dc:dc + 1],
                                                   in1=o_yac[0][:],
                                                   op0=OP.mult, op1=OP.subtract))
                cx.wait(ve, VE)
                cx.inc(VE, ve.scalar_tensor_tensor(out=o_y16[:], in0=o_t1[:], scalar=1.0,
                                                   in1=o_sil[bb], op0=OP.mult, op1=OP.mult))
                sil_free[0] = cx.val(VE)
                cx.wait(gp, VE)
                QS.dma(out=d_y1[:, dc, tloc:tloc + SLB], in_=o_y16[:])
                bo_free_ve[bb] = cx.val(VE)
                bo_free_sc[bb] = cx.val(SC)

        stage_barrier([TE, SC, VE])


        TE, SC, VE = "cTE", "cSC", "cVE"
        A32.reset(); A16.reset()
        c_x = A32.take(2 * DIM)
        c_x1p = A32.take(2 * DIM)
        c_fm32 = A32.take(NKC * SLAB)
        c_junk = A32.take(DIM)
        c_L = A32.take(64)
        c_cwfm = A32.take(SLAB)
        c_y1 = A16.take(NDC * SLAB)
        c_wo = [A16.take(DIM), A16.take(DIM)]
        c_fm16 = A16.take(NKC * SLAB)
        tsx = sb_tiny

        for sl in range(NS):
            t0 = sl * SLAB
            cx.wait(sy, TE); cx.wait(sy, VE); cx.wait(sy, SC)
            QL.dma(out=c_y1.rearrange("p (d c) -> p d c", d=NDC),
                                    in_=d_y1[:, :, t0:t0 + SLAB])
            QL.dma(out=c_x.rearrange("p (a c) -> p a c", a=2),
                                    in_=x_own[t0:t0 + SLAB, :].rearrange("(a p) c -> p a c", p=128))
            for k in range(16):
                wb = c_wo[k % 2]
                cx.wait(sy, TE)
                QL.dma(out=wb, in_=d_woutb[k])
                QL.wait_all(te)
                if k == 0:
                    cx.wait(te, VE)
                for tc in range(2):
                    for nh in range(2):
                        ins = te.matmul(pO[tc * 2 + nh][:, 0:512],
                                        c_y1[:, k * SLAB + tc * 128:k * SLAB + tc * 128 + 128],
                                        wb[:, nh * 512:(nh + 1) * 512],
                                        start=(k == 0), stop=(k == 15),
                                        skip_group_check=True)
                        if tc == 1 and nh == 1:
                            cx.inc(TE, ins)
            for tc in range(2):
                xtc = c_x[:, tc * DIM:(tc + 1) * DIM]
                x1tc = c_x1p[:, tc * DIM:(tc + 1) * DIM]
                cx.wait(ve, TE); QL.wait_all(ve); cx.wait(ve, SC)
                for nh in range(2):
                    cx.wait(ve, VE)
                    cx.inc(VE, ve.tensor_tensor(out=x1tc[:, nh * 512:(nh + 1) * 512],
                                                in0=pO[tc * 2 + nh][:, 0:512],
                                                in1=xtc[:, nh * 512:(nh + 1) * 512], op=OP.add))
                cx.wait(sc, VE); QS.wait_all(sc)
                cx.wait(sc, SC)
                cx.inc(SC, sc.activation(out=c_junk[:], in_=x1tc, func=AF.Square,
                                         accum_out=tsx[:, 0:1]))
                cx.wait(sc, SC)
                cx.inc(SC, sc.activation(out=c_junk[:], in_=x1tc, func=AF.Copy,
                                         accum_out=tsx[:, 1:2]))
                cx.wait(ve, SC)
                cx.wait(ve, VE)
                cx.inc(VE, ve.tensor_scalar(out=tsx[:, 2:3], in0=tsx[:, 1:2],
                                            scalar1=1.0 / DIM, scalar2=None, op0=OP.mult))
                cx.wait(ve, VE)
                cx.inc(VE, ve.tensor_tensor(out=tsx[:, 3:4], in0=tsx[:, 2:3],
                                            in1=tsx[:, 2:3], op=OP.mult))
                cx.wait(ve, VE)
                cx.inc(VE, ve.tensor_scalar(out=tsx[:, 4:5], in0=tsx[:, 3:4], scalar1=-1.0,
                                            scalar2=EPS, op0=OP.mult, op1=OP.add))
                cx.wait(sc, VE)
                cx.wait(sc, SC)
                cx.inc(SC, sc.activation(out=tsx[:, 14:15], in_=tsx[:, 0:1], func=AF.Sqrt,
                                         scale=1.0 / DIM, bias=tsx[:, 4:5]))
                cx.wait(ve, SC)
                cx.wait(ve, VE)
                cx.inc(VE, ve.reciprocal(out=tsx[:, 5:6], in_=tsx[:, 14:15]))
                cx.wait(ve, VE)
                cx.inc(VE, ve.tensor_tensor(out=tsx[:, 6:7], in0=tsx[:, 2:3],
                                            in1=tsx[:, 5:6], op=OP.mult))
                cx.wait(ve, VE)
                cx.inc(VE, ve.tensor_scalar(out=tsx[:, 7:8], in0=tsx[:, 6:7], scalar1=-1.0,
                                            scalar2=None, op0=OP.mult))
                cx.wait(sc, VE)
                cx.wait(sc, SC)
                cx.inc(SC, sc.activation(out=x1tc, in_=x1tc, func=AF.Identity,
                                         scale=tsx[:, 5:6], bias=tsx[:, 7:8]))
                cx.wait(gp, SC)
                QS.dma(out=d_x1[t0 + tc * 128: t0 + (tc + 1) * 128, :],
                                        in_=x1tc)
                for kc in range(NKC):
                    cx.wait(te, SC)
                    cx.inc(TE, te.transpose(pC[:, 0:128], x1tc[:, kc * 128:(kc + 1) * 128],
                                            sb_eye[:]))
                    cx.wait(sc, TE)
                    cx.wait(sc, SC)
                    cx.inc(SC, sc.activation(
                        out=c_fm32[:, kc * SLAB + tc * 128:kc * SLAB + tc * 128 + 128],
                        in_=pC[:, 0:128], func=AF.Copy))
            cx.wait(ve, SC); QS.wait_all(ve)
            cx.wait(ve, VE)
            cx.inc(VE, ve.tensor_copy(out=c_fm16[:], in_=c_fm32[:]))
            cx.wait(gp, VE)
            QS.dma(out=d_x1fm[:, :, t0:t0 + SLAB],
                                    in_=c_fm16.rearrange("p (k c) -> p k c", k=NKC))
            for tc in range(2):
                cx.wait(te, SC)
                for k in range(NKC):
                    ins = te.matmul(pD[:, 0:16],
                                    c_fm32[:, k * SLAB + tc * 128:k * SLAB + tc * 128 + 128],
                                    sb_wr[:, k * 16:(k + 1) * 16],
                                    start=(k == 0), stop=(k == NKC - 1))
                cx.inc(TE, ins)
                L = c_L[:, 0:16]
                cx.wait(sc, TE)
                cx.wait(sc, SC)
                cx.inc(SC, sc.activation(out=L, in_=pD[:, 0:16], func=AF.Copy))
                cx.wait(ve, SC)
                cx.wait(ve, VE)
                cx.inc(VE, ve.tensor_reduce(out=tsx[:, 8:9], in_=L, axis=mybir.AxisListType.X,
                                            op=OP.max))
                cx.wait(ve, VE)
                cx.inc(VE, ve.tensor_scalar(out=tsx[:, 9:10], in0=tsx[:, 8:9], scalar1=-1.0,
                                            scalar2=None, op0=OP.mult))
                cx.wait(sc, VE)
                cx.wait(sc, SC)
                cx.inc(SC, sc.activation(out=c_L[:, 16:32], in_=L, func=AF.Exp,
                                         bias=tsx[:, 9:10], accum_out=tsx[:, 10:11]))
                cx.wait(ve, SC)
                cx.wait(ve, VE)
                cx.inc(VE, ve.reciprocal(out=tsx[:, 11:12], in_=tsx[:, 10:11]))
                P = c_L[:, 32:48]
                cx.wait(ve, VE)
                cx.inc(VE, ve.tensor_scalar(out=P, in0=c_L[:, 16:32], scalar1=tsx[:, 11:12],
                                            scalar2=None, op0=OP.mult))
                cx.wait(ve, VE)
                cx.inc(VE, ve.tensor_reduce(out=tsx[:, 12:13], in_=P,
                                            axis=mybir.AxisListType.X, op=OP.max))
                EQ = c_L[:, 48:64]
                cx.wait(ve, VE)
                cx.inc(VE, ve.tensor_scalar(out=EQ, in0=P, scalar1=tsx[:, 12:13],
                                            scalar2=None, op0=OP.is_ge))
                cx.wait(ve, VE)
                cx.inc(VE, ve.tensor_scalar(out=EQ, in0=EQ, scalar1=-1.0, scalar2=1.0,
                                            op0=OP.mult, op1=OP.add))
                cx.wait(ve, VE)
                cx.inc(VE, ve.tensor_tensor(out=c_L[:, 16:32], in0=P, in1=EQ, op=OP.mult))
                cx.wait(ve, VE)
                cx.inc(VE, ve.tensor_reduce(out=tsx[:, 13:14], in_=c_L[:, 16:32],
                                            axis=mybir.AxisListType.X, op=OP.max))
                cx.wait(ve, VE)
                cx.inc(VE, ve.tensor_scalar(out=EQ, in0=P, scalar1=tsx[:, 13:14],
                                            scalar2=None, op0=OP.is_ge))
                cx.wait(ve, VE)
                cx.inc(VE, ve.tensor_tensor(out=P, in0=P, in1=EQ, op=OP.mult))
                cx.wait(te, VE)
                cx.inc(TE, te.transpose(pC[0:16, 0:128], P, sb_eye[:]))
                cx.wait(sc, TE)
                cx.wait(sc, SC)
                cx.inc(SC, sc.activation(out=c_cwfm[0:16, tc * 128:(tc + 1) * 128],
                                         in_=pC[0:16, 0:128], func=AF.Copy))
            cx.wait(gp, SC)
            QS.dma(out=d_cw[:, t0:t0 + SLAB], in_=c_cwfm[0:16, 0:SLAB])

        stage_barrier([TE, SC, VE])
        if debug:
            tmp = A32t[:, 0:DIM]
            for sl in range(NS):
                t0 = sl * SLAB
                for tc in range(2):
                    QS.wait_all(sy)
                    QL.dma(out=tmp, in_=d_x1[t0 + tc * 128:t0 + (tc + 1) * 128, :])
                    QL.wait_all(gp)
                    QS.dma(out=dbg["dbg_x1"][t0 + tc * 128:t0 + (tc + 1) * 128, :],
                                            in_=tmp)
                QS.wait_all(sy)
                QL.dma(out=tmp[0:16, 0:SLAB], in_=d_cw[:, t0:t0 + SLAB])
                QL.wait_all(gp)
                QS.dma(out=dbg["dbg_cw"][:, t0:t0 + SLAB],
                                        in_=tmp[0:16, 0:SLAB])
            stage_barrier([])

        TE, SC, VE = "dTE", "dSC", "dVE"
        SLD = 512
        NSD = T_OWN // SLD
        A32.reset(); A16.reset()
        g_cwf = A32.take(NST * SLD)
        g_cwr = [A32.take(SLD), A32.take(SLD)]
        g_sg2 = [A32.take(SLD), A32.take(SLD)]
        g_t1 = A32.take(SLD)
        gu_rel_sc = [0, 0]
        gu_rel_ve = [0, 0]
        g_x1 = A16.take(NKC * SLD)
        g_wt = [[A16.take(8 * 4 * 128), A16.take(8 * 4 * 128)],
                [A16.take(8 * 4 * 128), A16.take(8 * 4 * 128)]]
        g_z = A16.take(SLD)
        gwt_free = [[0, 0], [0, 0]]
        cwr_free = [0, 0]

        for sl in range(NSD):
            t0 = sl * SLD
            cx.wait(sy, TE); cx.wait(sy, VE)
            QL.dma(out=g_x1.rearrange("p (k c) -> p k c", k=NKC),
                   in_=d_x1fm[:, :, t0:t0 + SLD])
            QL.dma(out=g_cwf[0:1, :], in_=d_cw[:, t0:t0 + SLD])
            for pe in range(18):
                pp = pe % 2
                f0 = pe * 4
                for g in range(2):
                    cx.wait_val(sy, TE, gwt_free[g][pp])
                    QL.dma(out=g_wt[g][pp].rearrange("p (k f c) -> p k f c", k=8, f=4),
                           in_=d_wgub.rearrange("g p (k ff c) -> g p k ff c",
                                                k=8, ff=NF)[g, :, :, f0:f0 + 4, :])
                if pe < 16:
                    QL.wait_all(te); cx.wait_val(te, VE, cwr_free[pp])
                    cx.inc(TE, te.matmul(pC[:, 0:SLD], sb_ones[0:1, 0:128],
                                         g_cwf[0:1, pe * SLD:(pe + 1) * SLD],
                                         start=True, stop=True))
                    cx.wait(sc, TE); cx.wait(sc, SC)
                    cx.inc(SC, sc.activation(out=g_cwr[pp][:], in_=pC[:, 0:SLD], func=AF.Copy))
                for f in range(4):
                    fp = f % 2
                    pg = pO[fp * 2]
                    pu = pO[fp * 2 + 1]
                    QL.wait_all(te)
                    cx.wait_val(te, SC, gu_rel_sc[fp])
                    cx.wait_val(te, VE, gu_rel_ve[fp])
                    for k in range(NKC):
                        ins = te.matmul(pg[:, 0:SLD],
                                        g_wt[0][pp][:, (k * 4 + f) * 128:(k * 4 + f + 1) * 128],
                                        g_x1[:, k * SLD:(k + 1) * SLD],
                                        start=(k == 0), stop=(k == NKC - 1))
                    cx.inc(TE, ins)
                    cx.wait(sc, TE); cx.wait(sc, SC)
                    cx.inc(SC, sc.activation(out=g_sg2[fp][:], in_=pg[:, 0:SLD], func=AF.Silu))
                    gu_rel_sc[fp] = cx.val(SC)
                    for k in range(NKC):
                        ins = te.matmul(pu[:, 0:SLD],
                                        g_wt[1][pp][:, (k * 4 + f) * 128:(k * 4 + f + 1) * 128],
                                        g_x1[:, k * SLD:(k + 1) * SLD],
                                        start=(k == 0), stop=(k == NKC - 1))
                    cx.inc(TE, ins)
                    cx.wait(ve, TE); cx.wait(ve, SC); QS.wait_all(ve); cx.wait(ve, VE)
                    if pe < 16:
                        cx.inc(VE, ve.scalar_tensor_tensor(out=g_t1[:], in0=pu[:, 0:SLD],
                                                           scalar=1.0, in1=g_sg2[fp][:],
                                                           op0=OP.mult, op1=OP.mult))
                        cx.wait(ve, VE)
                        cx.inc(VE, ve.scalar_tensor_tensor(out=g_z[:], in0=g_t1[:], scalar=1.0,
                                                           in1=g_cwr[pp][:],
                                                           op0=OP.mult, op1=OP.mult))
                    else:
                        cx.inc(VE, ve.scalar_tensor_tensor(out=g_z[:], in0=pu[:, 0:SLD],
                                                           scalar=1.0, in1=g_sg2[fp][:],
                                                           op0=OP.mult, op1=OP.mult))
                    gu_rel_ve[fp] = cx.val(VE)
                    cx.wait(gp, VE)
                    QS.dma(out=d_z[:, f0 + f, t0:t0 + SLD], in_=g_z[:])
                gwt_free[0][pp] = cx.val(TE)
                gwt_free[1][pp] = cx.val(TE)
                cwr_free[pp] = cx.val(VE)

        stage_barrier([TE, SC, VE])


        TE, SC, VE = "eTE", "eSC", "eVE"
        A32.reset(); A16.reset()
        e_x1 = A32.take(2 * DIM)
        e_o = A32.take(2 * DIM)
        e_junk = A32.take(DIM)
        e_z = [A16.take(36 * SLAB), A16.take(36 * SLAB)]
        e_wd = [A16.take(2 * DIM), A16.take(2 * DIM)]
        ewd_free = [0, 0]
        ez_free = [0, 0]

        for sl in range(NS):
            t0 = sl * SLAB
            cx.wait(sy, VE); cx.wait(sy, SC)
            QL.dma(out=e_x1.rearrange("p (a c) -> p a c", a=2),
                                    in_=d_x1[t0:t0 + SLAB, :].rearrange("(a p) c -> p a c", p=128))
            for half in range(2):
                hb = (2 * sl + half) % 2
                cx.wait_val(sy, TE, ez_free[hb])
                QL.dma(out=e_z[hb].rearrange("p (k c) -> p k c", k=36),
                                        in_=d_z[:, half * 36:(half + 1) * 36, t0:t0 + SLAB])
                for kk in range(36):
                    k = half * 36 + kk
                    gi2 = (kk // 2) % 2
                    if kk % 2 == 0:
                        cx.wait_val(sy, TE, ewd_free[gi2])
                        QL.dma(out=e_wd[gi2].rearrange("p (a c) -> p a c", a=2),
                               in_=d_wdownb[k:k + 2].rearrange("a p c -> p a c"))
                    wb = e_wd[gi2][:, (kk % 2) * DIM:(kk % 2 + 1) * DIM]
                    QL.wait_all(te)
                    if half == 0 and kk == 0:
                        cx.wait(te, VE)
                    for tc in range(2):
                        for nh in range(2):
                            ins = te.matmul(pO[tc * 2 + nh][:, 0:512],
                                            e_z[hb][:, kk * SLAB + tc * 128:kk * SLAB + tc * 128 + 128],
                                            wb[:, nh * 512:(nh + 1) * 512],
                                            start=(half == 0 and kk == 0),
                                            stop=(half == 1 and kk == 35),
                                            skip_group_check=True)
                            if tc == 1 and nh == 1:
                                cx.inc(TE, ins)
                    if kk % 2 == 1:
                        ewd_free[gi2] = cx.val(TE)
                ez_free[hb] = cx.val(TE)
            for tc in range(2):
                x1tc = e_x1[:, tc * DIM:(tc + 1) * DIM]
                otc = e_o[:, tc * DIM:(tc + 1) * DIM]
                cx.wait(ve, TE); QL.wait_all(ve); cx.wait(ve, SC)
                for nh in range(2):
                    cx.wait(ve, VE)
                    cx.inc(VE, ve.tensor_tensor(out=otc[:, nh * 512:(nh + 1) * 512],
                                                in0=pO[tc * 2 + nh][:, 0:512],
                                                in1=x1tc[:, nh * 512:(nh + 1) * 512], op=OP.add))
                cx.wait(sc, VE); QS.wait_all(sc)
                cx.wait(sc, SC)
                cx.inc(SC, sc.activation(out=e_junk[:], in_=otc, func=AF.Square,
                                         accum_out=sb_tiny[:, 0:1]))
                cx.wait(sc, SC)
                cx.inc(SC, sc.activation(out=e_junk[:], in_=otc, func=AF.Copy,
                                         accum_out=sb_tiny[:, 1:2]))
                cx.wait(ve, SC)
                cx.wait(ve, VE)
                cx.inc(VE, ve.tensor_scalar(out=sb_tiny[:, 2:3], in0=sb_tiny[:, 1:2],
                                            scalar1=1.0 / DIM, scalar2=None, op0=OP.mult))
                cx.wait(ve, VE)
                cx.inc(VE, ve.tensor_tensor(out=sb_tiny[:, 3:4], in0=sb_tiny[:, 2:3],
                                            in1=sb_tiny[:, 2:3], op=OP.mult))
                cx.wait(ve, VE)
                cx.inc(VE, ve.tensor_scalar(out=sb_tiny[:, 4:5], in0=sb_tiny[:, 3:4],
                                            scalar1=-1.0, scalar2=EPS, op0=OP.mult, op1=OP.add))
                cx.wait(sc, VE)
                cx.wait(sc, SC)
                cx.inc(SC, sc.activation(out=sb_tiny[:, 14:15], in_=sb_tiny[:, 0:1],
                                         func=AF.Sqrt, scale=1.0 / DIM, bias=sb_tiny[:, 4:5]))
                cx.wait(ve, SC)
                cx.wait(ve, VE)
                cx.inc(VE, ve.reciprocal(out=sb_tiny[:, 5:6], in_=sb_tiny[:, 14:15]))
                cx.wait(ve, VE)
                cx.inc(VE, ve.tensor_tensor(out=sb_tiny[:, 6:7], in0=sb_tiny[:, 2:3],
                                            in1=sb_tiny[:, 5:6], op=OP.mult))
                cx.wait(ve, VE)
                cx.inc(VE, ve.tensor_scalar(out=sb_tiny[:, 7:8], in0=sb_tiny[:, 6:7],
                                            scalar1=-1.0, scalar2=None, op0=OP.mult))
                cx.wait(sc, VE)
                cx.wait(sc, SC)
                cx.inc(SC, sc.activation(out=otc, in_=otc, func=AF.Identity,
                                         scale=sb_tiny[:, 5:6], bias=sb_tiny[:, 7:8]))
                cx.wait(gp, SC)
                QS.dma(out=out[t0 + tc * 128:t0 + (tc + 1) * 128, :],
                                        in_=otc)

        QS.wait_all(gp)
        QS.wait_all(sy)

    es.close()
    return nc


def stage_inputs(inputs, T_OWN=4096, x_override=None):
    x = np.asarray(x_override if x_override is not None else inputs["L0_attn_x"])
    nseq = x.shape[0]
    win = np.asarray(inputs["L0_attn_in_proj_w"])
    conv_w = np.asarray(inputs["L0_attn_conv_w"]).reshape(DIN, 4)
    conv_b = np.asarray(inputs["L0_attn_conv_b"])
    wxp = np.asarray(inputs["L0_attn_x_proj_w"])
    wdt = np.asarray(inputs["L0_attn_dt_proj_w"])
    dtb = np.asarray(inputs["L0_attn_dt_proj_b"])
    alog = np.asarray(inputs["L0_attn_A_log"])
    dpar = np.asarray(inputs["L0_attn_D_param"])
    wout = np.asarray(inputs["L0_attn_out_proj_w"])
    wr = np.asarray(inputs["L0_ffn_Wr"])
    wg = np.asarray(inputs["L0_ffn_Wg"])
    wu = np.asarray(inputs["L0_ffn_Wu"])
    wd = np.asarray(inputs["L0_ffn_Wd"])
    sg = np.asarray(inputs["L0_ffn_Sg"])
    su = np.asarray(inputs["L0_ffn_Su"])
    sd = np.asarray(inputs["L0_ffn_Sd"])
    C = np.ascontiguousarray

    win_f = C(win.reshape(8, 128, 32, 128).transpose(1, 0, 2, 3)).reshape(128, 8 * 32 * 128)
    wout_f = C(wout.reshape(16, 128, DIM))
    wxp_f = C(wxp.reshape(16, 128, 96))
    wdt_f = C(wdt.reshape(64, 16 * 128))

    def gu_tiles(we, ws):
        e_t = we.reshape(16, 8, 128, 4, 128).transpose(2, 1, 0, 3, 4).reshape(128, 8, 64, 128)
        s_t = ws.reshape(8, 128, 8, 128).transpose(1, 0, 2, 3)
        return np.concatenate([e_t, s_t], axis=2)

    wgu_f = C(np.stack([gu_tiles(wg, sg), gu_tiles(wu, su)],
                       axis=0)).reshape(2, 128, 8 * NF * 128)
    wdown_f = C(np.concatenate([wd.reshape(64, 128, DIM), sd.reshape(8, 128, DIM)], axis=0))
    wr_f = C(wr.reshape(8, 128, 16))
    alog_t = C(alog.reshape(16, 128, 16).transpose(1, 0, 2)).reshape(128, 256)
    dpar_t = C(dpar.reshape(16, 128).transpose(1, 0))
    convw_t = C(conv_w.reshape(16, 128, 4).transpose(1, 0, 2)).reshape(128, 64)
    convb_t = C(conv_b.reshape(16, 128).transpose(1, 0))
    dtb_t = C(dtb.reshape(16, 128).transpose(1, 0))
    eye = np.eye(128, dtype=np.float32)

    shared = dict(win_f=win_f, wout_f=wout_f, wxp_f=wxp_f, wdt_f=wdt_f,
                  wgu_f=wgu_f, wdown_f=wdown_f, wr_f=wr_f, alog_t=alog_t,
                  dpar_t=dpar_t, convw_t=convw_t, convb_t=convb_t, dtb_t=dtb_t,
                  eye_in=eye)

    in_maps = []
    n_cores = 2 * nseq
    for c in range(n_cores):
        s = c // 2
        role_b = c % 2
        own = x[s, role_b * T_OWN:(role_b + 1) * T_OWN, :]
        xfm = np.zeros((NKC, 128, 2 * T_OWN + 8), dtype=np.float32)
        if role_b:
            xfm[:, :, 8:8 + T_OWN] = C(x[s, 0:T_OWN, :].T).reshape(NKC, 128, T_OWN)
        xfm[:, :, 8 + T_OWN:] = C(own.T).reshape(NKC, 128, T_OWN)
        m = dict(shared)
        m["x_fm_all"] = xfm
        m["x_own"] = C(own)
        m["flag"] = np.full((128, 1), float(role_b), dtype=np.float32)
        in_maps.append(m)
    return in_maps


def kernel(**inputs):
    T_OWN = 4096
    in_maps = stage_inputs(inputs, T_OWN)
    nc = build(T_OWN)
    res = run_bass_kernel_spmd(nc, in_maps, core_ids=list(range(8)))
    outp = np.empty((4, 8192, DIM), dtype=np.float32)
    for c in range(8):
        s = c // 2
        role_b = c % 2
        outp[s, role_b * T_OWN:(role_b + 1) * T_OWN, :] = res.results[c]["out"]
    return outp
